# version 14
# baseline (speedup 1.0000x reference)
import os
import sys
import numpy as np
import ml_dtypes
from contextlib import ExitStack

for _p in ("/opt/trn_rl_repo",):
    if os.path.isdir(_p) and _p not in sys.path:
        sys.path.insert(0, _p)

import concourse.bass as bass
import concourse.mybir as mybir
from concourse.bass_utils import run_bass_kernel_spmd

F32 = mybir.dt.float32
BF16 = mybir.dt.bfloat16
I32 = mybir.dt.int32
AF = mybir.ActivationFunctionType
ALU = mybir.AluOpType
AX = mybir.AxisListType

N_CORES = 8
NE = 3
MBLK = 32
EPS_LN = 1e-5


class Plan:
    ENGS = ("sync", "gpsimd", "vector", "scalar", "tensor")
    CHAIN = {"vector": "cv", "scalar": "cs2", "gpsimd": "sg"}

    def __init__(self):
        self.q = {e: [] for e in self.ENGS}
        self.cnt = {}
        self.semobj = {}

    def add_sem(self, nc, stack, name):
        self.semobj[name] = stack.enter_context(nc.semaphore(name))
        self.cnt[name] = 0

    def op(self, eng, fn, inc=None, amt=1, chain=True):
        ch = self.CHAIN.get(eng) if chain else None
        if ch is not None:
            assert inc is None
            prev = self.cnt[ch]
            chsem = self.semobj[ch]
            self.cnt[ch] += 1
            tok = self.cnt[ch]
            def wrapped(e, fn=fn, chsem=chsem, prev=prev):
                if prev > 0:
                    e.wait_ge(chsem, prev)
                return fn(e).then_inc(chsem, 1)
            self.q[eng].append(wrapped)
            return tok
        if inc is None:
            self.q[eng].append(fn)
            return None
        sem = self.semobj[inc]
        self.cnt[inc] += amt
        tok = self.cnt[inc]
        self.q[eng].append(lambda e, fn=fn, sem=sem, amt=amt: fn(e).then_inc(sem, amt))
        return tok

    def wait(self, eng, name, tok):
        if tok is None or tok <= 0:
            return
        sem = self.semobj[name]
        self.q[eng].append(lambda e, sem=sem, tok=tok: e.wait_ge(sem, tok))

    def dma(self, eng, out, in_, inc, slow=False):
        if slow:
            return self.op(eng, lambda e, out=out, in_=in_: e.dma_start(
                out=out, in_=in_, allow_slow_non_contiguous=True),
                inc=inc, amt=16, chain=False)
        return self.op(eng, lambda e, out=out, in_=in_: e.dma_start(out=out, in_=in_),
                       inc=inc, amt=16, chain=False)


class _StopPlan(Exception):
    pass


def build(D=4096, T=1024, S=4096, debug=False, stop_phase=None,
          a2_no_stats=False, a2_no_evac=False, a2_no_mult=False,
          a2_no_gen=False):
    NCH = D // 128
    NBLK = D // MBLK
    HT = T // 2
    FH2 = NCH // 2
    NTC = T // 128
    VIS = int(S * 0.3)
    R32 = 128 // MBLK
    assert NBLK <= 128 and HT <= 512 and NCH % 2 == 0

    nc = bass.Bass("TRN2", target_bir_lowering=False)

    x_fm = nc.dram_tensor("x_fm", [D, T], F32, kind="ExternalInput").ap()
    wtr = nc.dram_tensor("wtr", [NCH, 2, 128, FH2 * 128], F32, kind="ExternalInput").ap()
    maskd = nc.dram_tensor("maskd", [D // N_CORES, D], I32, kind="ExternalInput").ap()
    w1r = nc.dram_tensor("w1r", [NE, NCH, 2, 128, FH2 * 128], F32, kind="ExternalInput").ap()
    w2r = nc.dram_tensor("w2r", [NE, 2, NCH, 128, FH2 * 128], F32, kind="ExternalInput").ap()
    b1r = nc.dram_tensor("b1r", [128, NE, NCH], F32, kind="ExternalInput").ap()
    b2c = nc.dram_tensor("b2c", [128, NE, NCH], F32, kind="ExternalInput").ap()
    biasc = nc.dram_tensor("biasc", [128, NCH], F32, kind="ExternalInput").ap()
    gwr = nc.dram_tensor("gwr", [128, NCH, NE], F32, kind="ExternalInput").ap()
    gwlast = nc.dram_tensor("gwlast", [1, NE], F32, kind="ExternalInput").ap()
    gbias = nc.dram_tensor("gbias", [1, NE], F32, kind="ExternalInput").ap()
    visrow = nc.dram_tensor("visrow", [1, T], BF16, kind="ExternalInput").ap()
    mysel = nc.dram_tensor("mysel", [1, N_CORES], F32, kind="ExternalInput").ap()
    tok0sel = nc.dram_tensor("tok0sel", [1, N_CORES], F32, kind="ExternalInput").ap()
    out_fm = nc.dram_tensor("out_fm", [D, T], F32, kind="ExternalOutput").ap()

    x1_dram = nc.dram_tensor("x1_dram", [D, T], F32).ap()

    cc_in = nc.dram_tensor("cc_in", [1, N_CORES], F32)
    cc_out = nc.dram_tensor("cc_out", [1, N_CORES * N_CORES], F32)
    cc2_in = nc.dram_tensor("cc2_in", [D // MBLK // N_CORES, D // MBLK], F32)
    cc2_out = nc.dram_tensor("cc2_out", [D // MBLK, D // MBLK], F32)
    bounce = nc.dram_tensor("bounce", [4, 4], F32).ap()

    E = Plan()

    with ExitStack() as st_:
        def sbt(name, shape, dtype):
            return st_.enter_context(nc.sbuf_tensor(name, shape, dtype))

        big1 = sbt("big1", [128, NCH, T], BF16)
        big2 = sbt("big2", [128, NCH, T], BF16)
        pan = sbt("pan", [128, 2, FH2 * 128], BF16)
        ovl = sbt("ovl", [128, NCH * 512], BF16)
        bcb = sbt("bcb", [128, 5, T], BF16)
        sqst2 = sbt("sqst2", [128, 2, T], BF16)
        tmpb = sbt("tmpb", [128, T], BF16)
        tmpg = sbt("tmpg", [128, T], BF16)
        gw_sb = sbt("gw_sb", [128, NCH, NE], BF16)
        onec = sbt("onec", [128, 1], BF16)
        oner = sbt("oner", [128, 128], F32)
        idf = sbt("idf", [128, 128], F32)
        rows = sbt("rows", [1, 3 * T], F32)
        absrow_t = sbt("absrow_t", [1, T], BF16)
        b1sb = sbt("b1sb", [128, NE, NCH], F32)
        b2sb = sbt("b2sb", [128, NE, NCH], F32)
        bcsb = sbt("bcsb", [128, NCH], F32)
        lgtm = sbt("lgtm", [128, NTC, 4], F32)
        sc3 = sbt("sc3", [128, NTC, 4], F32)
        ctm = sbt("ctm", [128, NTC, 4], F32)
        gv = sbt("gv", [128, NTC, 8], F32)
        ohb = sbt("ohb", [128, 8], F32)
        srow = sbt("srow", [1, N_CORES * N_CORES], F32)
        svec = sbt("svec", [1, 48], F32)
        selrow = sbt("selrow", [1, 2 * N_CORES], F32)
        vtrow = sbt("vtrow", [1, T], BF16)
        corrc = sbt("corrc", [NE, 1], F32)
        bmt = sbt("bmt", [128, 128], F32)
        bmtb = sbt("bmtb", [128, 128], BF16)
        mred = sbt("mred", [128, 1, 128], F32)

        ovlf = ovl.ap()
        bcbf = bcb.ap().rearrange("p a t -> p (a t)")
        lg32 = bcbf[:, 0:2 * T].bitcast(F32)

        mexp_s = [ovlf[:, 0:NCH * 128], ovlf[:, NCH * 128:2 * NCH * 128]]
        pall_v = ovlf[:, 2 * NCH * 128:2 * NCH * 128 + D]
        call_v = ovlf[:, 2 * NCH * 128 + D:2 * NCH * 128 + 2 * D]
        act_v = ovlf[:, 0:FH2 * T].rearrange("p (c t) -> p c t", t=T)
        xstg = ovlf[:, 0:T * 8].bitcast(F32).rearrange("p (a t) -> p a t", t=T)

        assert D // 2 == 2 * T
        pan3 = bcbf[:, 0:2 * T]

        def pan_ap(sl):
            return pan[:, sl, :] if sl < 2 else pan3

        big2f = big2.ap().rearrange("p c t -> p (c t)")
        NMS = 2 if NCH * T >= 4 * D else 1
        mstage = [big2f[:, (2 * i) * D:(2 * i + 2) * D].bitcast(F32)
                  for i in range(NMS)]

        P = [st_.enter_context(nc.psum_tensor(f"P{i}", [128, 512], F32))
             for i in range(8)]

        for s in ("dxb0", "dxb1", "dxb2", "dxb3", "dmask0", "dmask1",
                  "dpan0", "dpan1", "dpan2", "dxs0", "dxs1", "dxs2", "dxs3",
                  "dq", "dgm", "ccs", "cv", "cs2", "st", "sg"):
            E.add_sem(nc, st_, s)
        C = E.cnt

        def maybe_stop(name):
            if stop_phase == name:
                raise _StopPlan()

        def eng_of(c):
            return ("gpsimd", "sg") if c % 3 == 2 else ("vector", "cv")

        mrow = rows[0:1, 0:T]
        rrow = rows[0:1, T:2 * T]
        tmprow = rows[0:1, 2 * T:3 * T]
        absrow = absrow_t[0:1, 0:T]

        try:
            E.op("gpsimd", lambda e: e.memset(onec[:, :], 1.0))
            E.op("gpsimd", lambda e: e.memset(oner[:, :], 1.0))
            tok_svz = E.op("gpsimd", lambda e: e.memset(svec[:, :], 0.0))
            E.op("gpsimd", lambda e: e.memset(idf[:, :], 0.0))
            E.op("gpsimd", lambda e: e.affine_select(
                out=idf[:, :], in_=idf[:, :], compare_op=ALU.not_equal, fill=1.0,
                base=0, pattern=[[-1, 128]], channel_multiplier=1))
            E.op("gpsimd", lambda e: e.memset(call_v, 0.0))
            tok_call = E.op("gpsimd", lambda e: e.affine_select(
                out=call_v.rearrange("p (a b) -> p a b", b=MBLK),
                in_=call_v.rearrange("p (a b) -> p a b", b=MBLK),
                compare_op=ALU.not_equal, fill=1.0, base=0,
                pattern=[[-1, D // MBLK], [0, MBLK]], channel_multiplier=1))
            tok_gw = E.dma("gpsimd", gw_sb[:, :, :], gwr[:, :, :], "dgm")
            E.wait("gpsimd", "dgm", tok_gw)
            E.dma("sync", b1sb[:, :, :], b1r[:, :, :], "dq")
            E.dma("sync", b2sb[:, :, :], b2c[:, :, :], "dq")
            E.dma("sync", bcsb[:, :], biasc[:, :], "dq")
            E.wait("sync", "sg", tok_svz)
            E.dma("sync", svec[0:1, 31:34], gwlast[:, :], "dq")
            E.dma("sync", svec[0:1, 34:37], gbias[:, :], "dq")
            E.dma("sync", selrow[0:1, 0:N_CORES], mysel[:, :], "dq")
            E.dma("sync", selrow[0:1, N_CORES:2 * N_CORES], tok0sel[:, :], "dq")
            tok_misc_q = E.dma("sync", vtrow[0:1, 0:T], visrow[:, :], "dq")

            msem = ("dmask0", "dmask1")
            mfree = [0, 0]
            tr_done = 0
            red2_done = 0
            xsem = ("dxb0", "dxb1", "dxb2", "dxb3")
            x_consumed = [0, 0, 0, 0]
            tok_xload = [None] * NCH
            sq_free = 0
            tok_a1_stats = 0
            NMC = D // 128 // N_CORES
            NIC = 4 * NMC

            def a1_chunk(c):
                nonlocal sq_free, tok_a1_stats
                sl = c % 4
                E.wait("scalar", xsem[sl], tok_xload[c])
                E.wait("scalar", "st", sq_free)
                tok_sq = E.op("scalar", lambda e, c=c: e.activation(
                    sqst2[:, c % 2, :], big1[:, c, :], AF.Square))
                E.wait("tensor", "cs2", tok_sq)
                for th in range(2):
                    E.op("tensor", lambda e, c=c, th=th: e.matmul(
                        P[0 + th][0:1, 0:HT], onec[:, :],
                        big1[:, c, th * HT:(th + 1) * HT],
                        start=(c == 0), stop=(c == NCH - 1)), inc="st")
                    tok_a1_stats = sq_free = E.op(
                        "tensor", lambda e, c=c, th=th: e.matmul(
                            P[2 + th][0:1, 0:HT], onec[:, :],
                            sqst2[:, c % 2, th * HT:(th + 1) * HT],
                            start=(c == 0), stop=(c == NCH - 1)), inc="st")
                x_consumed[sl] = tok_a1_stats

            for r in range(NMC):
                sl = r % NMS
                E.wait("gpsimd", "cv", mfree[sl])
                tok_l = E.dma("gpsimd", mstage[sl],
                              maskd[r * 128:(r + 1) * 128, :], msem[sl])
                E.wait("vector", msem[sl], tok_l)
                E.wait("vector", "st", tr_done)
                mfree[sl] = E.op("vector", lambda e, sl=sl: e.tensor_reduce(
                    mred[:, 0, 0:NBLK],
                    mstage[sl].rearrange("p (a b) -> p a b", b=MBLK),
                    AX.X, ALU.add))
                E.wait("tensor", "cv", mfree[sl])
                E.wait("tensor", "cv", red2_done)
                if r == 0:
                    E.wait("tensor", "sg", tok_call)
                tr_done = E.op("tensor", lambda e, sl=sl: e.transpose(
                    P[6][0:NBLK, 0:128], mred[:, 0, 0:NBLK], idf[:, :]),
                    inc="st")
                E.wait("vector", "st", tr_done)
                red2_done = E.op("vector", lambda e, r=r: e.tensor_reduce(
                    bmt[0:NBLK, R32 * r:R32 * (r + 1)],
                    P[6][0:NBLK, 0:128].rearrange("p (a b) -> p a b", b=MBLK),
                    AX.X, ALU.add))
            E.wait("tensor", "cv", red2_done)
            t_pc = E.op("tensor", lambda e: e.transpose(
                P[6][0:NIC, 0:NBLK], bmt[0:NBLK, 0:NIC], idf[0:NBLK, 0:NBLK]),
                inc="st")
            E.wait("vector", "st", t_pc)
            tok_pc = E.op("vector", lambda e: e.tensor_copy(
                mred[0:NIC, 0, 0:NBLK], P[6][0:NIC, 0:NBLK]))

            NXE = min(8, NCH - 1)
            for r in range(NCH):
                xsl = r % 4
                if r >= 4:
                    a1_chunk(r - 4)
                if r == NXE:
                    E.wait("gpsimd", "cv", tok_pc)
                    E.dma("gpsimd", cc2_in.ap(), mred[0:NIC, 0, 0:NBLK], "dgm")
                    E.wait("gpsimd", "dgm", C["dgm"])
                    tok_cc2 = E.op("gpsimd", lambda e: e.collective_compute(
                        "AllGather", ALU.bypass,
                        replica_groups=[list(range(N_CORES))],
                        ins=[cc2_in.ap().opt()], outs=[cc2_out.ap().opt()]),
                        inc="ccs", chain=False)
                E.wait("gpsimd", "st", x_consumed[xsl])
                tok_xload[r] = E.dma("gpsimd", big1[:, r, :],
                                     x_fm[r * 128:(r + 1) * 128, :], xsem[xsl])
            for rr in range(max(0, NCH - 4), NCH):
                a1_chunk(rr)

            E.wait("gpsimd", "ccs", tok_cc2)
            tok_bmload = E.dma("gpsimd", bmt[0:NBLK, 0:NBLK], cc2_out.ap(),
                               "dgm")
            E.wait("gpsimd", "dgm", tok_bmload)
            E.wait("tensor", "dgm", tok_bmload)
            t_bm = E.op("tensor", lambda e: e.transpose(
                P[6][0:NBLK, 0:NBLK], bmt[0:NBLK, 0:NBLK],
                idf[0:NBLK, 0:NBLK]), inc="st")
            E.wait("vector", "st", t_bm)
            tok_bmtb = E.op("vector", lambda e: e.tensor_scalar(
                bmtb[0:NBLK, 0:NBLK], P[6][0:NBLK, 0:NBLK], 0.0, None,
                ALU.is_gt))

            E.wait("tensor", "cv", tok_bmtb)
            tok_pall = 0
            for sp in range(D // 512):
                t_mm = E.op("tensor", lambda e, sp=sp: e.matmul(
                    P[7][0:NBLK, 0:512], bmtb[0:NBLK, 0:NBLK],
                    call_v[0:NBLK, sp * 512:(sp + 1) * 512],
                    start=True, stop=True), inc="st")
                E.wait("scalar", "st", t_mm)
                tok_pall = E.op("scalar", lambda e, sp=sp: e.copy(
                    pall_v[0:NBLK, sp * 512:(sp + 1) * 512],
                    P[7][0:NBLK, 0:512]))
                if sp < D // 512 - 1:
                    E.wait("tensor", "cs2", tok_pall)

            def stats_rows(tok_ready, banks):
                b0, b1, b2_, b3 = banks
                E.wait("vector", "st", tok_ready)
                E.op("vector", lambda e: e.tensor_scalar(
                    mrow[0:1, 0:HT], P[b0][0:1, 0:HT], 1.0 / D, None, ALU.mult))
                E.op("vector", lambda e: e.tensor_scalar(
                    mrow[0:1, HT:T], P[b1][0:1, 0:HT], 1.0 / D, None, ALU.mult))
                E.op("vector", lambda e: e.tensor_scalar(
                    rrow[0:1, 0:HT], P[b2_][0:1, 0:HT], 1.0 / D, None, ALU.mult))
                E.op("vector", lambda e: e.tensor_scalar(
                    rrow[0:1, HT:T], P[b3][0:1, 0:HT], 1.0 / D, None, ALU.mult))
                E.op("vector", lambda e: e.tensor_tensor(
                    tmprow[0:1, 0:T], mrow[0:1, 0:T], mrow[0:1, 0:T], ALU.mult))
                E.op("vector", lambda e: e.tensor_tensor(
                    rrow[0:1, 0:T], rrow[0:1, 0:T], tmprow[0:1, 0:T], ALU.subtract))
                tok = E.op("vector", lambda e: e.tensor_scalar(
                    rrow[0:1, 0:T], rrow[0:1, 0:T], EPS_LN, None, ALU.add))
                E.wait("scalar", "cv", tok)
                tok2 = E.op("scalar", lambda e: e.activation(
                    rrow[0:1, 0:T], rrow[0:1, 0:T], AF.Sqrt))
                E.wait("vector", "cs2", tok2)
                E.op("vector", lambda e: e.reciprocal(rrow[0:1, 0:T], rrow[0:1, 0:T]))
                tok3 = E.op("vector", lambda e: e.tensor_tensor(
                    mrow[0:1, 0:T], mrow[0:1, 0:T], rrow[0:1, 0:T], ALU.mult))
                return tok3

            bc_prev = [0]

            def bcast_row(rowap, dst_bf, tok_row):
                E.wait("tensor", "cv", tok_row)
                E.wait("tensor", "cs2", bc_prev[0])
                t1 = None
                for th in range(2):
                    t1 = E.op("tensor", lambda e, th=th: e.matmul(
                        (P[4] if th == 0 else P[5])[:, 0:HT],
                        oner[0:1, 0:128],
                        rowap[0:1, th * HT:(th + 1) * HT],
                        start=True, stop=True), inc="st")
                E.wait("scalar", "st", t1)
                t2 = None
                for th in range(2):
                    t2 = E.op("scalar", lambda e, th=th: e.copy(
                        dst_bf[:, th * HT:(th + 1) * HT],
                        (P[4] if th == 0 else P[5])[:, 0:HT]))
                bc_prev[0] = t2
                return t2

            maybe_stop("mask")
            tok_r = stats_rows(tok_a1_stats, (0, 1, 2, 3))
            bcast_row(mrow, bcb[:, 0, :], tok_r)
            tok_b1 = bcast_row(rrow, bcb[:, 1, :], tok_r)

            E.wait("vector", "cs2", tok_b1)
            E.wait("gpsimd", "cs2", tok_b1)
            h1_tok = {"cv": 0, "sg": 0}
            for c in range(NCH):
                eng, sem = eng_of(c)
                tb = tmpb if eng == "vector" else tmpg
                E.op(eng, lambda e, c=c, tb=tb: e.tensor_tensor(
                    tb[:, :], big1[:, c, :], bcb[:, 1, :], ALU.mult))
                h1_tok[sem] = E.op(eng, lambda e, c=c, tb=tb: e.tensor_tensor(
                    big1[:, c, :], tb[:, :], bcb[:, 0, :], ALU.subtract))
            tok_h1 = h1_tok["cv"]
            tok_h1g = h1_tok["sg"]

            pansem = ("dpan0", "dpan1", "dpan2")
            pan_used = [0, 0, 0]
            piece_ctr = [0]
            evac_pair = [0, 0]
            sqa_free = 0
            tok_attn_stats = 0
            E.wait("tensor", "cs2", tok_pall)
            E.wait("tensor", "cv", tok_h1)
            E.wait("tensor", "sg", tok_h1g)
            E.wait("vector", "dq", tok_misc_q)

            mexp_mult_done = [0, 0]
            gen_ev_of = {}
            genev_bank = {0: 0, 1: 0, 2: 0, 3: 0}
            gen_pending = []

            def gen_group(m, g):
                pr2 = m % 2
                bk = 2 * pr2 if g % 2 == 0 else 2 * pr2 + 1
                if g == 0:
                    E.wait("tensor", "cv", evac_pair[pr2])
                    if m >= 2:
                        E.wait("tensor", "cv", mexp_mult_done[pr2])
                E.wait("tensor", "cs2", genev_bank[bk])
                tg = None
                for j in range(4):
                    kc = 4 * g + j
                    tg = E.op("tensor",
                              lambda e, kc=kc, bk=bk, j=j, m=m: e.matmul(
                                  P[bk][:, j * 128:(j + 1) * 128],
                                  pall_v[0:NBLK, kc * 128:(kc + 1) * 128],
                                  call_v[0:NBLK, m * 128:(m + 1) * 128],
                                  start=True, stop=True),
                              inc="st" if j == 3 else None) or tg
                E.wait("scalar", "st", tg)
                genev_bank[bk] = E.op(
                    "scalar", lambda e, g=g, bk=bk, m=m: e.copy(
                        mexp_s[m % 2][:, g * 512:(g + 1) * 512],
                        P[bk][:, 0:512]))
                gen_ev_of[m] = genev_bank[bk]

            def emit_gen(m):
                for g in ([] if a2_no_gen else range(NCH // 4)):
                    gen_group(m, g)

            sq_tok_of = {}

            def a2_stats(m):
                nonlocal tok_attn_stats, sqa_free
                E.wait("tensor", "cs2", sq_tok_of[m])
                for th in range(2):
                    E.op("tensor", lambda e, m=m, th=th: e.matmul(
                        P[4 + th][0:1, 0:HT], onec[:, :],
                        big2[:, m, th * HT:(th + 1) * HT],
                        start=(m == 0), stop=(m == NCH - 1)), inc="st")
                    tok_attn_stats = sqa_free = E.op(
                        "tensor", lambda e, m=m, th=th: e.matmul(
                            P[6 + th][0:1, 0:HT], onec[:, :],
                            sqst2[:, m % 2, th * HT:(th + 1) * HT],
                            start=(m == 0), stop=(m == NCH - 1)), inc="st")

            emit_gen(0)
            for m in range(NCH):
                maybe_stop(f"A2m{m}")
                pr = m % 2
                bA = 2 * pr
                for h in range(2):
                    sl = piece_ctr[0] % 3
                    piece_ctr[0] += 1
                    E.wait("gpsimd", "st", pan_used[sl])
                    if sl == 2 and pan_used[2] == 0:
                        E.wait("gpsimd", "cv", tok_h1)
                        E.wait("gpsimd", "sg", tok_h1g)
                    tok_pd = E.dma("gpsimd", pan_ap(sl), wtr[m, h, :, :],
                                   pansem[sl])
                    E.wait("vector", pansem[sl], tok_pd)
                    E.wait("vector", "cs2", gen_ev_of[m])
                    if a2_no_mult:
                        tok_mm_ready = E.op("vector", lambda e, sl=sl:
                                            e.tensor_copy(tmpb[:, 0:8],
                                                          pan_ap(sl)[:, 0:8]))
                    else:
                        tok_mm_ready = E.op(
                            "vector", lambda e, sl=sl, h=h, m=m: e.tensor_tensor(
                                pan_ap(sl), pan_ap(sl),
                                mexp_s[m % 2][:, h * (FH2 * 128):
                                              (h + 1) * (FH2 * 128)], ALU.mult))
                    if h == 1:
                        mexp_mult_done[pr] = tok_mm_ready
                    E.wait("tensor", "cv", tok_mm_ready)
                    if h == 0:
                        E.wait("tensor", "cv", evac_pair[pr])
                    if h == 0 and m + 1 < NCH and not a2_no_gen:
                        gen_pending.extend((m + 1, g) for g in range(NCH // 4))
                    tok_done = None
                    nmm_p = 0
                    for kcp in range(FH2):
                        for th in range(2):
                            last = (kcp == FH2 - 1 and th == 1)
                            tok_done = E.op(
                                "tensor",
                                lambda e, kcp=kcp, th=th, sl=sl, h=h, bA=bA:
                                e.matmul(
                                    P[bA + th][:, 0:HT],
                                    pan_ap(sl)[:, kcp * 128:(kcp + 1) * 128],
                                    big1[:, h * FH2 + kcp,
                                         th * HT:(th + 1) * HT],
                                    start=(h == 0 and kcp == 0),
                                    stop=(h == 1 and kcp == FH2 - 1)),
                                inc="st" if last else None) or tok_done
                            nmm_p += 1
                            if nmm_p % 8 == 0 and gen_pending:
                                gm, gg = gen_pending.pop(0)
                                gen_group(gm, gg)
                    pan_used[sl] = tok_done
                    panel_last = tok_done
                if a2_no_evac:
                    continue
                E.wait("vector", "st", panel_last)
                for th in range(2):
                    evac_pair[pr] = E.op(
                        "vector", lambda e, m=m, th=th, bA=bA: e.tensor_scalar(
                            big2[:, m, th * HT:(th + 1) * HT],
                            P[bA + th][:, 0:HT],
                            bcsb[:, m:m + 1], None, ALU.add))
                if a2_no_stats:
                    continue
                E.wait("scalar", "cv", evac_pair[pr])
                E.wait("scalar", "st", sqa_free)
                sq_tok_of[m] = E.op("scalar", lambda e, m=m: e.activation(
                    sqst2[:, m % 2, :], big2[:, m, :], AF.Square))
                if m >= 1:
                    a2_stats(m - 1)
                if m == NCH - 1:
                    a2_stats(m)

            maybe_stop("A2")
            tok_r2 = stats_rows(tok_attn_stats, (4, 5, 6, 7))
            bcast_row(mrow, bcb[:, 0, :], tok_r2)
            tok_b1 = bcast_row(rrow, bcb[:, 1, :], tok_r2)

            xssem = ("dxs0", "dxs1", "dxs2", "dxs3")
            xs_tok = [0, 0, 0, 0]
            xs_scal = [0, 0, 0, 0]
            sq_free2 = 0
            tok_x1_stats = 0
            E.wait("vector", "cs2", tok_b1)
            E.wait("sync", "st", C["st"])
            E.wait("sync", "cv", C["cv"])
            E.wait("sync", "cs2", C["cs2"])
            E.wait("gpsimd", "cs2", tok_b1)
            for c in range(NCH):
                sl = c % 4
                eng, sem = eng_of(c)
                tb = tmpb if eng == "vector" else tmpg
                E.wait("sync", "cs2", xs_scal[sl])
                E.wait("sync", xssem[sl], xs_tok[sl])
                tok_xl = E.dma("sync", xstg[:, sl, :], x_fm[c * 128:(c + 1) * 128, :],
                               xssem[sl])
                E.op(eng, lambda e, c=c, tb=tb: e.tensor_tensor(
                    tb[:, :], big2[:, c, :], bcb[:, 1, :], ALU.mult))
                E.op(eng, lambda e, c=c, tb=tb: e.tensor_tensor(
                    tb[:, :], tb[:, :], bcb[:, 0, :], ALU.subtract))
                E.wait(eng, xssem[sl], tok_xl)
                tok_x1 = E.op(eng, lambda e, c=c, sl=sl, tb=tb: e.tensor_tensor(
                    xstg[:, sl, :], xstg[:, sl, :], tb[:, :], ALU.add))
                E.wait("sync", sem, tok_x1)
                xs_tok[sl] = E.dma("sync", x1_dram[c * 128:(c + 1) * 128, :],
                                   xstg[:, sl, :], xssem[sl])
                E.wait("scalar", sem, tok_x1)
                xs_scal[sl] = E.op("scalar", lambda e, c=c, sl=sl: e.copy(
                    big1[:, c, :], xstg[:, sl, :]))
                E.wait("scalar", "st", sq_free2)
                tok_sq = E.op("scalar", lambda e, c=c: e.activation(
                    sqst2[:, c % 2, :], big1[:, c, :], AF.Square))
                E.wait("tensor", "cs2", tok_sq)
                for th in range(2):
                    E.op("tensor", lambda e, c=c, th=th: e.matmul(
                        P[4 + th][0:1, 0:HT], onec[:, :],
                        big1[:, c, th * HT:(th + 1) * HT],
                        start=(c == 0), stop=(c == NCH - 1)), inc="st")
                    tok_x1_stats = sq_free2 = E.op(
                        "tensor", lambda e, c=c, th=th: e.matmul(
                            P[6 + th][0:1, 0:HT], onec[:, :],
                            sqst2[:, c % 2, th * HT:(th + 1) * HT],
                            start=(c == 0), stop=(c == NCH - 1)), inc="st")
            xs_after_p1 = list(xs_tok)
            maybe_stop("A3p1")

            tok_r3 = stats_rows(tok_x1_stats, (4, 5, 6, 7))
            bcast_row(mrow, bcb[:, 0, :], tok_r3)
            tok_b1 = bcast_row(rrow, bcb[:, 1, :], tok_r3)

            sq_free3 = 0
            tok_gate = 0
            E.wait("vector", "cs2", tok_b1)
            E.wait("tensor", "dgm", tok_bmload)
            E.wait("gpsimd", "cs2", tok_b1)
            for c in range(NCH):
                eng, sem = eng_of(c)
                tb = tmpb if eng == "vector" else tmpg
                E.op(eng, lambda e, c=c, tb=tb: e.tensor_tensor(
                    tb[:, :], big1[:, c, :], bcb[:, 1, :], ALU.mult))
                tok_h2 = E.op(eng, lambda e, c=c, tb=tb: e.tensor_tensor(
                    big1[:, c, :], tb[:, :], bcb[:, 0, :], ALU.subtract))
                E.wait("scalar", "st", sq_free3)
                E.wait("scalar", sem, tok_h2)
                tok_abs = E.op("scalar", lambda e, c=c: e.activation(
                    sqst2[:, c % 2, :], big1[:, c, :], AF.Abs))
                E.wait("tensor", "cs2", tok_abs)
                for th in range(2):
                    sq_free3 = E.op("tensor", lambda e, c=c, th=th: e.matmul(
                        P[4 + th][0:1, 0:HT], onec[:, :],
                        sqst2[:, c % 2, th * HT:(th + 1) * HT],
                        start=(c == 0), stop=(c == NCH - 1)), inc="st")
                    tok_gate = E.op("tensor", lambda e, c=c, th=th: e.matmul(
                        P[6 + th][0:NE, 0:HT], gw_sb[:, c, :],
                        big1[:, c, th * HT:(th + 1) * HT],
                        start=(c == 0), stop=(c == NCH - 1)), inc="st")

            h2_last = {"cv": C["cv"], "sg": C["sg"]}
            E.wait("vector", "st", tok_gate)
            E.op("vector", lambda e: e.tensor_copy(absrow[0:1, 0:HT], P[4][0:1, 0:HT]))
            E.op("vector", lambda e: e.tensor_copy(absrow[0:1, HT:T], P[5][0:1, 0:HT]))
            E.op("vector", lambda e: e.tensor_copy(lg32[0:NE, 0:HT], P[6][0:NE, 0:HT]))
            tok_lg = E.op("vector", lambda e: e.tensor_copy(
                lg32[0:NE, HT:T], P[7][0:NE, 0:HT]))

            E.op("vector", lambda e: e.tensor_tensor(
                tmprow[0:1, 0:T], absrow[0:1, 0:T], vtrow[0:1, 0:T], ALU.mult))
            E.op("vector", lambda e: e.tensor_reduce(
                svec[0:1, 0:1], tmprow[0:1, 0:T], AX.X, ALU.add))
            E.op("vector", lambda e: e.tensor_reduce(
                svec[0:1, 1:2], absrow[0:1, 0:T], AX.X, ALU.add))
            tok_pay_vt = E.op("vector", lambda e: e.tensor_tensor(
                svec[0:1, 1:2], svec[0:1, 1:2], svec[0:1, 0:1], ALU.subtract))

            E.wait("sync", "cv", tok_lg)
            E.dma("sync", bounce[0:NE, 0:1], lg32[0:NE, 0:1], "dq", slow=True)
            E.wait("sync", "dq", C["dq"])
            tok_t0 = E.dma("sync", svec[0:1, 2:2 + NE],
                           bounce[0:NE, 0:1].rearrange("a b -> b a"), "dq", slow=True)

            E.wait("gpsimd", "cv", tok_pay_vt)
            E.wait("gpsimd", "dq", tok_t0)
            E.dma("gpsimd", cc_in.ap(), svec[0:1, 0:N_CORES], "dgm")
            E.wait("gpsimd", "dgm", C["dgm"])
            tok_cc = E.op("gpsimd", lambda e: e.collective_compute(
                "AllGather", ALU.bypass, replica_groups=[list(range(N_CORES))],
                ins=[cc_in.ap().opt()], outs=[cc_out.ap().opt()]), inc="ccs", chain=False)
            E.wait("gpsimd", "ccs", tok_cc)
            tok_gath = E.dma("gpsimd", srow[0:1, :], cc_out.ap(), "dgm")

            gcell = srow.ap()[0:1, :].rearrange("p (r c) -> p r c", c=N_CORES)
            E.wait("vector", "dgm", tok_gath)
            E.wait("vector", "dq", C["dq"])
            E.op("vector", lambda e: e.tensor_tensor(
                svec[0:1, 8:16], gcell[:, :, 0], selrow[0:1, 0:N_CORES], ALU.mult))
            E.op("vector", lambda e: e.tensor_reduce(
                svec[0:1, 37:38], svec[0:1, 8:16], AX.X, ALU.add))
            E.op("vector", lambda e: e.tensor_tensor(
                svec[0:1, 8:16], gcell[:, :, 1], selrow[0:1, 0:N_CORES], ALU.mult))
            E.op("vector", lambda e: e.tensor_reduce(
                svec[0:1, 38:39], svec[0:1, 8:16], AX.X, ALU.add))
            E.op("vector", lambda e: e.tensor_scalar(
                svec[0:1, 39:40], svec[0:1, 37:38], 1.0 / (VIS * D), 1e-8,
                ALU.mult, ALU.add))
            E.op("vector", lambda e: e.reciprocal(svec[0:1, 39:40], svec[0:1, 39:40]))
            E.op("vector", lambda e: e.tensor_scalar(
                svec[0:1, 40:41], svec[0:1, 38:39], 1.0 / ((S - VIS) * D), None,
                ALU.mult))
            E.op("vector", lambda e: e.tensor_tensor(
                svec[0:1, 40:41], svec[0:1, 40:41], svec[0:1, 39:40], ALU.mult))
            E.op("vector", lambda e: e.scalar_tensor_tensor(
                svec[0:1, 16:19], svec[0:1, 31:34], svec[0:1, 40:41],
                svec[0:1, 34:37], ALU.mult, ALU.add))
            for e_ in range(NE):
                E.op("vector", lambda e, e_=e_: e.tensor_tensor(
                    svec[0:1, 8:16], gcell[:, :, 2 + e_],
                    selrow[0:1, N_CORES:2 * N_CORES], ALU.mult))
                E.op("vector", lambda e, e_=e_: e.tensor_reduce(
                    svec[0:1, 19 + e_:20 + e_], svec[0:1, 8:16], AX.X, ALU.add))
            E.op("vector", lambda e: e.tensor_tensor(
                svec[0:1, 19:22], svec[0:1, 19:22], svec[0:1, 16:19], ALU.add))

            g0, g1, g2 = (svec[0:1, 19 + i:20 + i] for i in range(3))
            t8 = [svec[0:1, 8 + i:9 + i] for i in range(8)]

            def vv(dst, a, b, op):
                E.op("vector", lambda e, dst=dst, a=a, b=b, op=op:
                     e.tensor_tensor(dst, a, b, op))

            def one_minus(dst, a):
                E.op("vector", lambda e, dst=dst, a=a: e.tensor_scalar(
                    dst, a, -1.0, 1.0, ALU.mult, ALU.add))

            vv(t8[0], g0, g1, ALU.is_ge)
            vv(t8[1], g0, g2, ALU.is_ge)
            vv(t8[2], g1, g2, ALU.is_ge)
            vv(svec[0:1, 22:23], t8[0], t8[1], ALU.mult)
            one_minus(t8[3], t8[0])
            vv(svec[0:1, 23:24], t8[3], t8[2], ALU.mult)
            one_minus(t8[4], t8[1])
            one_minus(t8[5], t8[2])
            vv(svec[0:1, 24:25], t8[4], t8[5], ALU.mult)
            vv(t8[0], g0, g1, ALU.is_lt)
            vv(t8[1], g0, g2, ALU.is_lt)
            vv(svec[0:1, 25:26], t8[0], t8[1], ALU.mult)
            vv(t8[2], g1, g0, ALU.is_le)
            vv(t8[3], g1, g2, ALU.is_lt)
            vv(svec[0:1, 26:27], t8[2], t8[3], ALU.mult)
            vv(t8[4], g2, g0, ALU.is_le)
            vv(t8[5], g2, g1, ALU.is_le)
            vv(svec[0:1, 27:28], t8[4], t8[5], ALU.mult)
            vv(svec[0:1, 8:11], svec[0:1, 22:25], svec[0:1, 25:28], ALU.add)
            tok_oh = E.op("vector", lambda e: e.tensor_scalar(
                svec[0:1, 28:31], svec[0:1, 8:11], -1.0, 1.0, ALU.mult, ALU.add))

            E.wait("sync", "cv", tok_oh)
            E.dma("sync", bounce[0:1, 0:NE], svec[0:1, 16:19], "dq")
            E.wait("sync", "dq", C["dq"])
            tok_cc2 = E.dma("sync", corrc[0:NE, 0:1],
                            bounce[0:1, 0:NE].rearrange("a b -> b a"), "dq", slow=True)
            E.wait("vector", "dq", tok_cc2)
            tok_lgc = E.op("vector", lambda e: e.tensor_scalar(
                lg32[0:NE, :], lg32[0:NE, :], corrc[0:NE, 0:1], None, ALU.add))

            E.op("vector", lambda e: e.memset(gv.ap().rearrange("p a b -> p (a b)"),
                                              0.0))
            tok_pz = E.op("vector", lambda e: e.memset(P[0][:, 0:4 * NTC], 0.0))
            E.wait("tensor", "cv", tok_lgc)
            E.wait("tensor", "cv", tok_pz)
            t_tr = None
            for tc in range(NTC):
                t_tr = E.op("tensor", lambda e, tc=tc: e.transpose(
                    P[0][:, 4 * tc:4 * tc + NE],
                    lg32[0:NE, tc * 128:(tc + 1) * 128], idf[0:NE, 0:NE]), inc="st")
            E.wait("vector", "st", t_tr)
            E.op("vector", lambda e: e.tensor_copy(
                lgtm.ap().rearrange("p a b -> p (a b)")[:, 0:4 * NTC],
                P[0][:, 0:4 * NTC]))

            E.op("vector", lambda e: e.tensor_reduce(
                gv[:, :, 0], lgtm[:, :, 0:NE], AX.X, ALU.max))
            for e_ in range(NE):
                E.op("vector", lambda e, e_=e_: e.tensor_tensor(
                    lgtm[:, :, e_], lgtm[:, :, e_], gv[:, :, 0], ALU.subtract))
            tok_sm1 = E.op("vector", lambda e: e.tensor_copy(
                sc3.ap().rearrange("p a b -> p (a b)"),
                lgtm.ap().rearrange("p a b -> p (a b)")))
            E.wait("scalar", "cv", tok_sm1)
            tok_exp = E.op("scalar", lambda e: e.activation(
                sc3.ap().rearrange("p a b -> p (a b)"),
                sc3.ap().rearrange("p a b -> p (a b)"), AF.Exp))
            E.wait("vector", "cs2", tok_exp)
            E.op("vector", lambda e: e.tensor_reduce(
                gv[:, :, 1], sc3[:, :, 0:NE], AX.X, ALU.add))
            E.op("vector", lambda e: e.reciprocal(gv[:, :, 2], gv[:, :, 1]))
            for e_ in range(NE):
                E.op("vector", lambda e, e_=e_: e.tensor_tensor(
                    sc3[:, :, e_], sc3[:, :, e_], gv[:, :, 2], ALU.mult))
            E.op("vector", lambda e: e.tensor_reduce(
                gv[:, :, 3], sc3[:, :, 0:NE], AX.X, ALU.max))
            E.op("vector", lambda e: e.tensor_reduce(
                gv[:, :, 4], sc3[:, :, 0:NE], AX.X, ALU.min))
            E.op("vector", lambda e: e.tensor_reduce(
                gv[:, :, 5], sc3[:, :, 0:NE], AX.X, ALU.add))
            E.op("vector", lambda e: e.tensor_tensor(
                gv[:, :, 5], gv[:, :, 5], gv[:, :, 3], ALU.subtract))
            E.op("vector", lambda e: e.tensor_tensor(
                gv[:, :, 5], gv[:, :, 5], gv[:, :, 4], ALU.subtract))
            E.op("vector", lambda e: e.tensor_tensor(
                gv[:, :, 6], gv[:, :, 3], gv[:, :, 5], ALU.add))
            E.op("vector", lambda e: e.tensor_scalar(
                gv[:, :, 6], gv[:, :, 6], 1e-8, None, ALU.add))
            E.op("vector", lambda e: e.reciprocal(gv[:, :, 6], gv[:, :, 6]))
            E.op("vector", lambda e: e.tensor_tensor(
                gv[:, :, 3], gv[:, :, 3], gv[:, :, 6], ALU.mult))
            tok_tvn = E.op("vector", lambda e: e.tensor_tensor(
                gv[:, :, 5], gv[:, :, 5], gv[:, :, 6], ALU.mult))

            E.wait("tensor", "cv", tok_oh)
            E.op("tensor", lambda e: e.matmul(
                P[1][:, 0:NE], oner[0:1, 0:128], svec[0:1, 22:25],
                start=True, stop=True), inc="st")
            t_ohmm = E.op("tensor", lambda e: e.matmul(
                P[1][:, 4:4 + NE], oner[0:1, 0:128], svec[0:1, 28:31],
                start=True, stop=True), inc="st")
            E.wait("vector", "st", t_ohmm)
            tok_ohb = E.op("vector", lambda e: e.tensor_copy(
                ohb[:, 0:8], P[1][:, 0:8]))

            tok_ctm = 0
            for e_ in range(NE):
                E.op("vector", lambda e, e_=e_: e.tensor_scalar(
                    ctm[:, :, 3], gv[:, :, 3], ohb[:, e_:e_ + 1], None, ALU.mult))
                tok_ctm = E.op("vector", lambda e, e_=e_: e.scalar_tensor_tensor(
                    ctm[:, :, e_], gv[:, :, 5], ohb[:, 4 + e_:5 + e_],
                    ctm[:, :, 3], ALU.mult, ALU.add))

            crow = [mrow, rrow, tmprow]
            E.wait("tensor", "cv", tok_ctm)
            tok_coeffbc = 0
            for e_ in range(NE):
                for base in range(0, NTC, 4):
                    ks = list(range(base, min(base + 4, NTC)))
                    t_tr2 = None
                    for k, tc in enumerate(ks):
                        t_tr2 = E.op("tensor", lambda e, tc=tc, k=k, e_=e_: e.transpose(
                            P[2][0:1, 128 * k:128 * (k + 1)],
                            ctm[:, tc, e_:e_ + 1], idf[:, :]), inc="st")
                    E.wait("vector", "st", t_tr2)
                    tok_cr = E.op("vector", lambda e, e_=e_, base=base, n=len(ks):
                                  e.tensor_copy(
                                      crow[e_][0:1, base * 128:base * 128 + n * 128],
                                      P[2][0:1, 0:n * 128]))
                    E.wait("tensor", "cv", tok_cr)
                for th in range(2):
                    t_bc = E.op("tensor", lambda e, e_=e_, th=th: e.matmul(
                        P[3][:, 0:HT], oner[0:1, 0:128],
                        crow[e_][0:1, th * HT:(th + 1) * HT],
                        start=True, stop=True), inc="st")
                    E.wait("scalar", "st", t_bc)
                    tok_coeffbc = E.op("scalar", lambda e, e_=e_, th=th: e.copy(
                        bcb[:, 2 + e_, th * HT:(th + 1) * HT], P[3][:, 0:HT]))
                    E.wait("tensor", "cs2", tok_coeffbc)

            maybe_stop("gate")
            FUNCS = (AF.Tanh, AF.Gelu, AF.Silu)
            pan_used = [C["st"], C["st"], 0]
            pc3 = [0]
            pan3_init = [False]
            E.wait("tensor", "cv", C["cv"])
            E.wait("tensor", "cs2", C["cs2"])
            E.wait("scalar", "dq", tok_misc_q)
            act_ss = {}
            moe_sv = {0: 0, 1: 0}
            first_moe = True
            for _sl in range(4):
                E.wait("scalar", xssem[_sl], xs_after_p1[_sl])
            for e_ in range(NE):
                for fh in range(2):
                    act_ss.clear()
                    for i in range(FH2):
                        fp = fh * FH2 + i
                        pr = i % 2
                        for h in range(2):
                            sl = pc3[0] % 3
                            pc3[0] += 1
                            E.wait("gpsimd", "st", pan_used[sl])
                            if sl == 2 and not pan3_init[0]:
                                pan3_init[0] = True
                                E.wait("gpsimd", "cv", h2_last["cv"])
                                E.wait("gpsimd", "sg", h2_last["sg"])
                            tok_pd = E.dma("gpsimd", pan_ap(sl),
                                           w1r[e_, fp, h, :, :], pansem[sl])
                            E.wait("tensor", pansem[sl], tok_pd)
                            if h == 0:
                                if i >= 2:
                                    E.wait("tensor", "cs2", act_ss[i - 2])
                                else:
                                    E.wait("tensor", "cv", moe_sv[pr])
                            tok_done = None
                            for dcp in range(FH2):
                                for th in range(2):
                                    last = (dcp == FH2 - 1 and th == 1)
                                    tok_done = E.op(
                                        "tensor",
                                        lambda e, dcp=dcp, th=th, sl=sl, h=h, pr=pr:
                                        e.matmul(
                                            P[2 * pr + th][:, 0:HT],
                                            pan_ap(sl)[:, dcp * 128:(dcp + 1) * 128],
                                            big1[:, h * FH2 + dcp,
                                                 th * HT:(th + 1) * HT],
                                            start=(h == 0 and dcp == 0),
                                            stop=(h == 1 and dcp == FH2 - 1)),
                                        inc="st" if last else None) or tok_done
                            pan_used[sl] = tok_done
                            panel_last = tok_done
                        E.wait("scalar", "st", panel_last)
                        t_act = None
                        for th in range(2):
                            t_act = E.op(
                                "scalar",
                                lambda e, i=i, th=th, e_=e_, pr=pr, fp=fp: e.activation(
                                    act_v[:, i, th * HT:(th + 1) * HT],
                                    P[2 * pr + th][:, 0:HT], FUNCS[e_],
                                    bias=b1sb[:, e_, fp:fp + 1], scale=1.0))
                        act_ss[i] = t_act
                        E.wait("vector", "cs2", t_act)
                        act_ready = E.op("vector", lambda e, i=i, e_=e_: e.tensor_tensor(
                            act_v[:, i, :], act_v[:, i, :], bcb[:, 2 + e_, :],
                            ALU.mult))
                    last_pass = (e_ == NE - 1 and fh == 1)
                    sqc_free = 0
                    tok_moe_stats = 0
                    E.wait("tensor", "cv", act_ready)

                    def moe_stats_chunk(j):
                        nonlocal sqc_free, tok_moe_stats
                        E.wait("scalar", "cv", b2_tok[j])
                        E.wait("scalar", "st", sqc_free)
                        tok_sq = E.op("scalar", lambda e, j=j: e.activation(
                            sqst2[:, j % 2, :], big2[:, j, :], AF.Square))
                        E.wait("tensor", "cs2", tok_sq)
                        for th in range(2):
                            E.op("tensor", lambda e, j=j, th=th: e.matmul(
                                P[4 + th][0:1, 0:HT], onec[:, :],
                                big2[:, j, th * HT:(th + 1) * HT],
                                start=(j == 0), stop=(j == NCH - 1)), inc="st")
                            tok_moe_stats = sqc_free = E.op(
                                "tensor", lambda e, j=j, th=th: e.matmul(
                                    P[6 + th][0:1, 0:HT], onec[:, :],
                                    sqst2[:, j % 2, th * HT:(th + 1) * HT],
                                    start=(j == 0), stop=(j == NCH - 1)),
                                inc="st")

                    b2_tok = {}
                    for j in range(NCH):
                        pr = j % 2
                        sl = pc3[0] % 3
                        pc3[0] += 1
                        E.wait("gpsimd", "st", pan_used[sl])
                        tok_pd = E.dma("gpsimd", pan_ap(sl),
                                       w2r[e_, fh, j, :, :], pansem[sl])
                        E.wait("tensor", pansem[sl], tok_pd)
                        if j >= 2:
                            E.wait("tensor", "cv", moe_sv[pr])
                        tok_done = None
                        for fcp in range(FH2):
                            for th in range(2):
                                last = (fcp == FH2 - 1 and th == 1)
                                tok_done = E.op(
                                    "tensor",
                                    lambda e, fcp=fcp, th=th, sl=sl, pr=pr: e.matmul(
                                        P[2 * pr + th][:, 0:HT],
                                        pan_ap(sl)[:, fcp * 128:(fcp + 1) * 128],
                                        act_v[:, fcp, th * HT:(th + 1) * HT],
                                        start=(fcp == 0), stop=(fcp == FH2 - 1)),
                                    inc="st" if last else None) or tok_done
                        pan_used[sl] = tok_done
                        if last_pass and j >= 1:
                            moe_stats_chunk(j - 1)
                        E.wait("vector", "st", tok_done)
                        for th in range(2):
                            if first_moe:
                                moe_sv[pr] = E.op(
                                    "vector",
                                    lambda e, j=j, th=th, pr=pr: e.tensor_scalar(
                                        big2[:, j, th * HT:(th + 1) * HT],
                                        P[2 * pr + th][:, 0:HT],
                                        0.0, None, ALU.add))
                            else:
                                moe_sv[pr] = E.op(
                                    "vector",
                                    lambda e, j=j, th=th, pr=pr: e.tensor_tensor(
                                        big2[:, j, th * HT:(th + 1) * HT],
                                        P[2 * pr + th][:, 0:HT],
                                        big2[:, j, th * HT:(th + 1) * HT],
                                        ALU.add))
                        if last_pass:
                            for eb in range(NE):
                                b2_tok[j] = E.op(
                                    "vector",
                                    lambda e, eb=eb, j=j: e.scalar_tensor_tensor(
                                        big2[:, j, :], bcb[:, 2 + eb, :],
                                        b2sb[:, eb, j:j + 1],
                                        big2[:, j, :], ALU.mult, ALU.add))
                    first_moe = False
            moe_stats_chunk(NCH - 1)

            if debug:
                NTC_ = T // 128
                dbg_moe = nc.dram_tensor("dbg_moe", [128, NCH * T], BF16).ap()
                dbg_coeff = nc.dram_tensor("dbg_coeff", [128, 3 * T], BF16).ap()
                dbg_lg = nc.dram_tensor("dbg_lg", [NE, T], F32).ap()
                dbg_gv = nc.dram_tensor("dbg_gv", [128, NTC_ * 8], F32).ap()
                dbg_svec = nc.dram_tensor("dbg_svec", [1, 48], F32).ap()
                dbg_ctm = nc.dram_tensor("dbg_ctm", [128, NTC_ * 4], F32).ap()
                dbg_act = nc.dram_tensor("dbg_act", [128, FH2 * T], BF16).ap()
                dbg_h2 = nc.dram_tensor("dbg_h2", [128, NCH * T], BF16).ap()
                E.wait("sync", "cv", C["cv"])
                E.dma("sync", dbg_moe, big2f[:, :], "dq")
                E.dma("sync", dbg_coeff,
                      bcb.ap().rearrange("p a t -> p (a t)")[:, 2 * T:5 * T], "dq")
                E.dma("sync", dbg_lg, lg32[0:NE, :], "dq")
                E.dma("sync", dbg_gv, gv.ap().rearrange("p a b -> p (a b)"), "dq")
                E.dma("sync", dbg_svec, svec[:, :], "dq")
                E.dma("sync", dbg_ctm, ctm.ap().rearrange("p a b -> p (a b)"), "dq")
                E.dma("sync", dbg_act, ovlf[:, 0:FH2 * T], "dq")
                E.dma("sync", dbg_h2,
                      big1.ap().rearrange("p c t -> p (c t)"), "dq")

            maybe_stop("B")
            tok_r4 = stats_rows(tok_moe_stats, (4, 5, 6, 7))
            bcast_row(mrow, bcb[:, 0, :], tok_r4)
            tok_b1 = bcast_row(rrow, bcb[:, 1, :], tok_r4)

            xs_out = list(xs_after_p1)
            E.wait("vector", "cs2", tok_b1)
            E.wait("sync", "st", C["st"])
            E.wait("sync", "cv", C["cv"])
            E.wait("sync", "cs2", C["cs2"])
            E.wait("gpsimd", "cs2", tok_b1)
            for c in range(NCH):
                sl = c % 4
                eng, sem = eng_of(c)
                tb = tmpb if eng == "vector" else tmpg
                E.wait("sync", xssem[sl], xs_out[sl])
                tok_xl = E.dma("sync", xstg[:, sl, :],
                               x1_dram[c * 128:(c + 1) * 128, :], xssem[sl])
                E.op(eng, lambda e, c=c, tb=tb: e.tensor_tensor(
                    tb[:, :], big2[:, c, :], bcb[:, 1, :], ALU.mult))
                E.op(eng, lambda e, c=c, tb=tb: e.tensor_tensor(
                    tb[:, :], tb[:, :], bcb[:, 0, :], ALU.subtract))
                E.wait(eng, xssem[sl], tok_xl)
                tok_o = E.op(eng, lambda e, c=c, sl=sl, tb=tb: e.tensor_tensor(
                    xstg[:, sl, :], xstg[:, sl, :], tb[:, :], ALU.add))
                E.wait("sync", sem, tok_o)
                xs_out[sl] = E.dma("sync", out_fm[c * 128:(c + 1) * 128, :],
                                   xstg[:, sl, :], xssem[sl])
            for _sl in range(4):
                E.wait("sync", xssem[_sl], xs_out[_sl])


        except _StopPlan:
            dump = nc.dram_tensor("dump", [128, NCH * T], BF16).ap()
            for name in ("cv", "cs2", "sg"):
                E.wait("sync", name, E.cnt[name])
            E.wait("sync", "st", E.cnt["st"])
            E.wait("gpsimd", "cv", E.cnt["cv"])
            tok_dmp = E.dma("gpsimd", dump,
                            big1.ap().rearrange("p c t -> p (c t)"), "dgm")
            E.wait("gpsimd", "dgm", E.cnt["dgm"])
            zs = E.op("gpsimd", lambda e: e.memset(xstg[:, 0, :], 0.0))
            E.wait("sync", "sg", zs)
            tokf = [None, None]
            for c in range(NCH):
                tokf[c % 2] = E.dma("sync", out_fm[c * 128:(c + 1) * 128, :],
                                    xstg[:, 0, :], ("dxs0", "dxs1")[c % 2])
            E.wait("sync", "dxs0", tokf[0])
            E.wait("sync", "dxs1", tokf[1])

        block = st_.enter_context(nc.Block())

        @block.sync
        def _(eng):
            for f in E.q["sync"]:
                f(eng)

        @block.gpsimd
        def _(eng):
            for f in E.q["gpsimd"]:
                f(eng)

        @block.vector
        def _(eng):
            for f in E.q["vector"]:
                f(eng)

        @block.scalar
        def _(eng):
            for f in E.q["scalar"]:
                f(eng)

        @block.tensor
        def _(eng):
            for f in E.q["tensor"]:
                f(eng)

    return nc


def _prep_inputs(L0_attn_data, L0_attn_mask, L0_attn_weight, L0_attn_bias,
                 L0_ffn_gate_w, L0_ffn_gate_b, L0_ffn_w1, L0_ffn_b1,
                 L0_ffn_w2, L0_ffn_b2, D, T, S):
    NCH = D // 128
    FH2 = NCH // 2
    VIS = int(S * 0.3)
    cpb = S // T

    x = np.ascontiguousarray(np.asarray(L0_attn_data, np.float32)).reshape(-1, D)
    assert x.shape[0] == N_CORES * T

    w = np.asarray(L0_attn_weight, np.float32)
    wtr = np.ascontiguousarray(
        w.reshape(NCH, 128, 2, FH2, 128).transpose(0, 2, 4, 3, 1)
    ).reshape(NCH, 2, 128, FH2 * 128)

    w1 = np.asarray(L0_ffn_w1, np.float32)
    w1r = np.ascontiguousarray(
        w1.reshape(NE, 2, FH2, 128, NCH, 128).transpose(0, 4, 1, 3, 2, 5)
    ).reshape(NE, NCH, 2, 128, FH2 * 128)

    w2 = np.asarray(L0_ffn_w2, np.float32)
    w2r = np.ascontiguousarray(
        w2.reshape(NE, 2, FH2, 128, NCH, 128).transpose(0, 1, 4, 3, 2, 5)
    ).reshape(NE, 2, NCH, 128, FH2 * 128)

    b1 = np.asarray(L0_ffn_b1, np.float32)
    b1r = np.ascontiguousarray(b1.reshape(NE, NCH, 128).transpose(2, 0, 1))
    b2 = np.asarray(L0_ffn_b2, np.float32)
    b2cv = np.ascontiguousarray(b2.reshape(NE, NCH, 128).transpose(2, 0, 1))
    bias = np.asarray(L0_attn_bias, np.float32)
    biascv = np.ascontiguousarray(bias.reshape(NCH, 128).T)

    gw = np.asarray(L0_ffn_gate_w, np.float32)
    gwr = np.ascontiguousarray(gw[:D].reshape(NCH, 128, NE).transpose(1, 0, 2))
    gwlast = np.ascontiguousarray(gw[D:D + 1, :])
    gb = np.ascontiguousarray(np.asarray(L0_ffn_gate_b, np.float32)[None, :])

    mask = np.ascontiguousarray(np.asarray(L0_attn_mask, np.int32))

    shared = dict(wtr=wtr, w1r=w1r, w2r=w2r, b1r=b1r, b2c=b2cv,
                  biasc=biascv, gwr=gwr, gwlast=gwlast, gbias=gb)

    in_maps = []
    for c in range(N_CORES):
        shard = x[c * T:(c + 1) * T, :]
        x_fm = np.ascontiguousarray(shard.T)
        s0 = (c % cpb) * T
        spos = s0 + np.arange(T)
        vis = (spos < VIS).astype(ml_dtypes.bfloat16)[None, :]
        mys = np.zeros((1, N_CORES), np.float32)
        mys[0, (c // cpb) * cpb:(c // cpb + 1) * cpb] = 1.0
        t0s = np.zeros((1, N_CORES), np.float32)
        t0s[0, (c // cpb) * cpb] = 1.0
        m = dict(shared)
        m.update(x_fm=x_fm, visrow=np.ascontiguousarray(vis),
                 maskd=mask[c * (D // N_CORES):(c + 1) * (D // N_CORES)],
                 mysel=mys, tok0sel=t0s)
        in_maps.append(m)
    return in_maps


def run(inputs, D, T, S, nc=None):
    in_maps = _prep_inputs(D=D, T=T, S=S, **inputs)
    if nc is None:
        nc = build(D=D, T=T, S=S)
    res = run_bass_kernel_spmd(nc, in_maps, core_ids=list(range(N_CORES)))
    shards = [np.asarray(res.results[c]["out_fm"], np.float32).T
              for c in range(N_CORES)]
    return np.concatenate(shards, axis=0)


def kernel(**inputs):
    D, T, S = 4096, 1024, 4096
    out = run(inputs, D, T, S)
    B = inputs["L0_attn_data"].shape[0]
    return np.ascontiguousarray(out.reshape(B, S, D).astype(np.float32))


# version 15
# speedup vs baseline: 1.1012x; 1.1012x over previous
import os
import sys
import numpy as np
import ml_dtypes
from contextlib import ExitStack

for _p in ("/opt/trn_rl_repo",):
    if os.path.isdir(_p) and _p not in sys.path:
        sys.path.insert(0, _p)

import concourse.bass as bass
import concourse.mybir as mybir
from concourse.bass_utils import run_bass_kernel_spmd

F32 = mybir.dt.float32
BF16 = mybir.dt.bfloat16
I32 = mybir.dt.int32
AF = mybir.ActivationFunctionType
ALU = mybir.AluOpType
AX = mybir.AxisListType

N_CORES = 8
NE = 3
MBLK = 32
EPS_LN = 1e-5


class Plan:
    ENGS = ("sync", "gpsimd", "vector", "scalar", "tensor")
    CHAIN = {"vector": "cv", "scalar": "cs2", "gpsimd": "sg"}

    def __init__(self):
        self.q = {e: [] for e in self.ENGS}
        self.cnt = {}
        self.semobj = {}

    def add_sem(self, nc, stack, name):
        self.semobj[name] = stack.enter_context(nc.semaphore(name))
        self.cnt[name] = 0

    def op(self, eng, fn, inc=None, amt=1, chain=True):
        ch = self.CHAIN.get(eng) if chain else None
        if ch is not None:
            assert inc is None
            prev = self.cnt[ch]
            chsem = self.semobj[ch]
            self.cnt[ch] += 1
            tok = self.cnt[ch]
            def wrapped(e, fn=fn, chsem=chsem, prev=prev):
                if prev > 0:
                    e.wait_ge(chsem, prev)
                return fn(e).then_inc(chsem, 1)
            self.q[eng].append(wrapped)
            return tok
        if inc is None:
            self.q[eng].append(fn)
            return None
        sem = self.semobj[inc]
        self.cnt[inc] += amt
        tok = self.cnt[inc]
        self.q[eng].append(lambda e, fn=fn, sem=sem, amt=amt: fn(e).then_inc(sem, amt))
        return tok

    def wait(self, eng, name, tok):
        if tok is None or tok <= 0:
            return
        sem = self.semobj[name]
        self.q[eng].append(lambda e, sem=sem, tok=tok: e.wait_ge(sem, tok))

    def dma(self, eng, out, in_, inc, slow=False):
        if slow:
            return self.op(eng, lambda e, out=out, in_=in_: e.dma_start(
                out=out, in_=in_, allow_slow_non_contiguous=True),
                inc=inc, amt=16, chain=False)
        return self.op(eng, lambda e, out=out, in_=in_: e.dma_start(out=out, in_=in_),
                       inc=inc, amt=16, chain=False)


class _StopPlan(Exception):
    pass


def build(D=4096, T=1024, S=4096, debug=False, stop_phase=None,
          a2_no_stats=False, a2_no_evac=False, a2_no_mult=False,
          a2_no_gen=False):
    NCH = D // 128
    NBLK = D // MBLK
    HT = T // 2
    FH2 = NCH // 2
    NTC = T // 128
    VIS = int(S * 0.3)
    R32 = 128 // MBLK
    assert NBLK <= 128 and HT <= 512 and NCH % 2 == 0

    nc = bass.Bass("TRN2", target_bir_lowering=False)

    x_fm = nc.dram_tensor("x_fm", [D, T], F32, kind="ExternalInput").ap()
    wtr = nc.dram_tensor("wtr", [NCH, 2, 128, FH2 * 128], F32, kind="ExternalInput").ap()
    maskd = nc.dram_tensor("maskd", [D // N_CORES, D], F32, kind="ExternalInput").ap()
    w1r = nc.dram_tensor("w1r", [NE, NCH, 2, 128, FH2 * 128], F32, kind="ExternalInput").ap()
    w2r = nc.dram_tensor("w2r", [NE, 2, NCH, 128, FH2 * 128], F32, kind="ExternalInput").ap()
    b1r = nc.dram_tensor("b1r", [128, NE, NCH], F32, kind="ExternalInput").ap()
    b2c = nc.dram_tensor("b2c", [128, NE, NCH], F32, kind="ExternalInput").ap()
    biasc = nc.dram_tensor("biasc", [128, NCH], F32, kind="ExternalInput").ap()
    gwr = nc.dram_tensor("gwr", [128, NCH, NE], F32, kind="ExternalInput").ap()
    gwlast = nc.dram_tensor("gwlast", [1, NE], F32, kind="ExternalInput").ap()
    gbias = nc.dram_tensor("gbias", [1, NE], F32, kind="ExternalInput").ap()
    visrow = nc.dram_tensor("visrow", [1, T], BF16, kind="ExternalInput").ap()
    mysel = nc.dram_tensor("mysel", [1, N_CORES], F32, kind="ExternalInput").ap()
    tok0sel = nc.dram_tensor("tok0sel", [1, N_CORES], F32, kind="ExternalInput").ap()
    out_fm = nc.dram_tensor("out_fm", [D, T], F32, kind="ExternalOutput").ap()

    x1_dram = nc.dram_tensor("x1_dram", [D, T], F32).ap()

    cc_in = nc.dram_tensor("cc_in", [1, N_CORES], F32)
    cc_out = nc.dram_tensor("cc_out", [1, N_CORES * N_CORES], F32)
    cc2_in = nc.dram_tensor("cc2_in", [D // MBLK // N_CORES, D // MBLK], F32)
    cc2_out = nc.dram_tensor("cc2_out", [D // MBLK, D // MBLK], F32)
    bounce = nc.dram_tensor("bounce", [4, 4], F32).ap()

    E = Plan()

    with ExitStack() as st_:
        def sbt(name, shape, dtype):
            return st_.enter_context(nc.sbuf_tensor(name, shape, dtype))

        big1 = sbt("big1", [128, NCH, T], BF16)
        big2 = sbt("big2", [128, NCH, T], BF16)
        pan = sbt("pan", [128, 2, FH2 * 128], BF16)
        ovl = sbt("ovl", [128, NCH * 512], BF16)
        bcb = sbt("bcb", [128, 5, T], BF16)
        sqst2 = sbt("sqst2", [128, 2, T], BF16)
        tmpb = sbt("tmpb", [128, T], BF16)
        tmpg = sbt("tmpg", [128, T], BF16)
        gw_sb = sbt("gw_sb", [128, NCH, NE], BF16)
        onec = sbt("onec", [128, 1], BF16)
        oner = sbt("oner", [128, 128], F32)
        idf = sbt("idf", [128, 128], F32)
        rows = sbt("rows", [1, 3 * T], F32)
        absrow_t = sbt("absrow_t", [1, T], BF16)
        b1sb = sbt("b1sb", [128, NE, NCH], F32)
        b2sb = sbt("b2sb", [128, NE, NCH], F32)
        bcsb = sbt("bcsb", [128, NCH], F32)
        lgtm = sbt("lgtm", [128, NTC, 4], F32)
        sc3 = sbt("sc3", [128, NTC, 4], F32)
        ctm = sbt("ctm", [128, NTC, 4], F32)
        gv = sbt("gv", [128, NTC, 8], F32)
        ohb = sbt("ohb", [128, 8], F32)
        srow = sbt("srow", [1, N_CORES * N_CORES], F32)
        svec = sbt("svec", [1, 48], F32)
        selrow = sbt("selrow", [1, 2 * N_CORES], F32)
        vtrow = sbt("vtrow", [1, T], BF16)
        corrc = sbt("corrc", [NE, 1], F32)
        bmt = sbt("bmt", [128, 128], F32)
        bmtb = sbt("bmtb", [128, 128], BF16)
        mred = sbt("mred", [128, 1, 128], F32)

        ovlf = ovl.ap()
        bcbf = bcb.ap().rearrange("p a t -> p (a t)")
        lg32 = bcbf[:, 0:2 * T].bitcast(F32)

        mexp_s = [ovlf[:, 0:NCH * 128], ovlf[:, NCH * 128:2 * NCH * 128]]
        pall_v = ovlf[:, 2 * NCH * 128:2 * NCH * 128 + D]
        call_v = ovlf[:, 2 * NCH * 128 + D:2 * NCH * 128 + 2 * D]
        act_v = ovlf[:, 0:FH2 * T].rearrange("p (c t) -> p c t", t=T)
        xstg = ovlf[:, 0:T * 8].bitcast(F32).rearrange("p (a t) -> p a t", t=T)

        assert D // 2 == 2 * T
        pan3 = bcbf[:, 0:2 * T]

        def pan_ap(sl):
            return pan[:, sl, :] if sl < 2 else pan3

        big2f = big2.ap().rearrange("p c t -> p (c t)")
        NMS = 2 if NCH * T >= 4 * D else 1
        mstage = [big2f[:, (2 * i) * D:(2 * i + 2) * D].bitcast(F32)
                  for i in range(NMS)]

        P = [st_.enter_context(nc.psum_tensor(f"P{i}", [128, 512], F32))
             for i in range(8)]

        for s in ("dxb0", "dxb1", "dxb2", "dxb3", "dmask0", "dmask1",
                  "dpan0", "dpan1", "dpan2", "dxs0", "dxs1", "dxs2", "dxs3",
                  "dq", "dgm", "ccs", "cv", "cs2", "st", "sg"):
            E.add_sem(nc, st_, s)
        C = E.cnt

        def maybe_stop(name):
            if stop_phase == name:
                raise _StopPlan()

        def eng_of(c):
            return ("gpsimd", "sg") if c % 3 == 2 else ("vector", "cv")

        mrow = rows[0:1, 0:T]
        rrow = rows[0:1, T:2 * T]
        tmprow = rows[0:1, 2 * T:3 * T]
        absrow = absrow_t[0:1, 0:T]

        try:
            E.op("gpsimd", lambda e: e.memset(onec[:, :], 1.0))
            E.op("gpsimd", lambda e: e.memset(oner[:, :], 1.0))
            tok_svz = E.op("gpsimd", lambda e: e.memset(svec[:, :], 0.0))
            E.op("gpsimd", lambda e: e.memset(idf[:, :], 0.0))
            E.op("gpsimd", lambda e: e.affine_select(
                out=idf[:, :], in_=idf[:, :], compare_op=ALU.not_equal, fill=1.0,
                base=0, pattern=[[-1, 128]], channel_multiplier=1))
            E.op("gpsimd", lambda e: e.memset(call_v, 0.0))
            tok_call = E.op("gpsimd", lambda e: e.affine_select(
                out=call_v.rearrange("p (a b) -> p a b", b=MBLK),
                in_=call_v.rearrange("p (a b) -> p a b", b=MBLK),
                compare_op=ALU.not_equal, fill=1.0, base=0,
                pattern=[[-1, D // MBLK], [0, MBLK]], channel_multiplier=1))
            tok_gw = E.dma("gpsimd", gw_sb[:, :, :], gwr[:, :, :], "dgm")
            E.wait("gpsimd", "dgm", tok_gw)
            E.dma("sync", b1sb[:, :, :], b1r[:, :, :], "dq")
            E.dma("sync", b2sb[:, :, :], b2c[:, :, :], "dq")
            E.dma("sync", bcsb[:, :], biasc[:, :], "dq")
            E.wait("sync", "sg", tok_svz)
            E.dma("sync", svec[0:1, 31:34], gwlast[:, :], "dq")
            E.dma("sync", svec[0:1, 34:37], gbias[:, :], "dq")
            E.dma("sync", selrow[0:1, 0:N_CORES], mysel[:, :], "dq")
            E.dma("sync", selrow[0:1, N_CORES:2 * N_CORES], tok0sel[:, :], "dq")
            tok_misc_q = E.dma("sync", vtrow[0:1, 0:T], visrow[:, :], "dq")

            msem = ("dmask0", "dmask1")
            mfree = [0, 0]
            tr_done = 0
            red2_done = 0
            xsem = ("dxb0", "dxb1", "dxb2", "dxb3")
            x_consumed = [0, 0, 0, 0]
            tok_xload = [None] * NCH
            sq_free = 0
            tok_a1_stats = 0
            NMC = D // 128 // N_CORES
            NIC = 4 * NMC

            def a1_chunk(c):
                nonlocal sq_free, tok_a1_stats
                sl = c % 4
                E.wait("scalar", xsem[sl], tok_xload[c])
                E.wait("scalar", "st", sq_free)
                tok_sq = E.op("scalar", lambda e, c=c: e.activation(
                    sqst2[:, c % 2, :], big1[:, c, :], AF.Square))
                E.wait("tensor", "cs2", tok_sq)
                for th in range(2):
                    E.op("tensor", lambda e, c=c, th=th: e.matmul(
                        P[0 + th][0:1, 0:HT], onec[:, :],
                        big1[:, c, th * HT:(th + 1) * HT],
                        start=(c == 0), stop=(c == NCH - 1)), inc="st")
                    tok_a1_stats = sq_free = E.op(
                        "tensor", lambda e, c=c, th=th: e.matmul(
                            P[2 + th][0:1, 0:HT], onec[:, :],
                            sqst2[:, c % 2, th * HT:(th + 1) * HT],
                            start=(c == 0), stop=(c == NCH - 1)), inc="st")
                x_consumed[sl] = tok_a1_stats

            for r in range(NMC):
                sl = r % NMS
                E.wait("sync", "cv", mfree[sl])
                tok_l = E.dma("sync", mstage[sl],
                              maskd[r * 128:(r + 1) * 128, :], msem[sl])
                E.wait("vector", msem[sl], tok_l)
                E.wait("vector", "st", tr_done)
                mfree[sl] = E.op("vector", lambda e, sl=sl: e.tensor_reduce(
                    mred[:, 0, 0:NBLK],
                    mstage[sl].rearrange("p (a b) -> p a b", b=MBLK),
                    AX.X, ALU.add))
                E.wait("tensor", "cv", mfree[sl])
                E.wait("tensor", "cv", red2_done)
                if r == 0:
                    E.wait("tensor", "sg", tok_call)
                tr_done = E.op("tensor", lambda e, sl=sl: e.transpose(
                    P[6][0:NBLK, 0:128], mred[:, 0, 0:NBLK], idf[:, :]),
                    inc="st")
                E.wait("vector", "st", tr_done)
                red2_done = E.op("vector", lambda e, r=r: e.tensor_reduce(
                    bmt[0:NBLK, R32 * r:R32 * (r + 1)],
                    P[6][0:NBLK, 0:128].rearrange("p (a b) -> p a b", b=MBLK),
                    AX.X, ALU.add))
            E.wait("tensor", "cv", red2_done)
            t_pc = E.op("tensor", lambda e: e.transpose(
                P[6][0:NIC, 0:NBLK], bmt[0:NBLK, 0:NIC], idf[0:NBLK, 0:NBLK]),
                inc="st")
            E.wait("vector", "st", t_pc)
            tok_pc = E.op("vector", lambda e: e.tensor_copy(
                mred[0:NIC, 0, 0:NBLK], P[6][0:NIC, 0:NBLK]))

            NXE = min(8, NCH - 1)
            for r in range(NCH):
                xsl = r % 4
                if r >= 4:
                    a1_chunk(r - 4)
                if r == NXE:
                    E.wait("gpsimd", "cv", tok_pc)
                    E.dma("gpsimd", cc2_in.ap(), mred[0:NIC, 0, 0:NBLK], "dgm")
                    E.wait("gpsimd", "dgm", C["dgm"])
                    tok_cc2 = E.op("gpsimd", lambda e: e.collective_compute(
                        "AllGather", ALU.bypass,
                        replica_groups=[list(range(N_CORES))],
                        ins=[cc2_in.ap().opt()], outs=[cc2_out.ap().opt()]),
                        inc="ccs", chain=False)
                E.wait("gpsimd", "st", x_consumed[xsl])
                tok_xload[r] = E.dma("gpsimd", big1[:, r, :],
                                     x_fm[r * 128:(r + 1) * 128, :], xsem[xsl])
            for rr in range(max(0, NCH - 4), NCH):
                a1_chunk(rr)

            E.wait("gpsimd", "ccs", tok_cc2)
            tok_bmload = E.dma("gpsimd", bmt[0:NBLK, 0:NBLK], cc2_out.ap(),
                               "dgm")
            E.wait("gpsimd", "dgm", tok_bmload)
            E.wait("tensor", "dgm", tok_bmload)
            t_bm = E.op("tensor", lambda e: e.transpose(
                P[6][0:NBLK, 0:NBLK], bmt[0:NBLK, 0:NBLK],
                idf[0:NBLK, 0:NBLK]), inc="st")
            E.wait("vector", "st", t_bm)
            tok_bmtb = E.op("vector", lambda e: e.tensor_scalar(
                bmtb[0:NBLK, 0:NBLK], P[6][0:NBLK, 0:NBLK], 0.0, None,
                ALU.is_gt))

            E.wait("tensor", "cv", tok_bmtb)
            tok_pall = 0
            for sp in range(D // 512):
                t_mm = E.op("tensor", lambda e, sp=sp: e.matmul(
                    P[7][0:NBLK, 0:512], bmtb[0:NBLK, 0:NBLK],
                    call_v[0:NBLK, sp * 512:(sp + 1) * 512],
                    start=True, stop=True), inc="st")
                E.wait("scalar", "st", t_mm)
                tok_pall = E.op("scalar", lambda e, sp=sp: e.copy(
                    pall_v[0:NBLK, sp * 512:(sp + 1) * 512],
                    P[7][0:NBLK, 0:512]))
                if sp < D // 512 - 1:
                    E.wait("tensor", "cs2", tok_pall)

            def stats_rows(tok_ready, banks):
                b0, b1, b2_, b3 = banks
                E.wait("vector", "st", tok_ready)
                E.op("vector", lambda e: e.tensor_scalar(
                    mrow[0:1, 0:HT], P[b0][0:1, 0:HT], 1.0 / D, None, ALU.mult))
                E.op("vector", lambda e: e.tensor_scalar(
                    mrow[0:1, HT:T], P[b1][0:1, 0:HT], 1.0 / D, None, ALU.mult))
                E.op("vector", lambda e: e.tensor_scalar(
                    rrow[0:1, 0:HT], P[b2_][0:1, 0:HT], 1.0 / D, None, ALU.mult))
                E.op("vector", lambda e: e.tensor_scalar(
                    rrow[0:1, HT:T], P[b3][0:1, 0:HT], 1.0 / D, None, ALU.mult))
                E.op("vector", lambda e: e.tensor_tensor(
                    tmprow[0:1, 0:T], mrow[0:1, 0:T], mrow[0:1, 0:T], ALU.mult))
                E.op("vector", lambda e: e.tensor_tensor(
                    rrow[0:1, 0:T], rrow[0:1, 0:T], tmprow[0:1, 0:T], ALU.subtract))
                tok = E.op("vector", lambda e: e.tensor_scalar(
                    rrow[0:1, 0:T], rrow[0:1, 0:T], EPS_LN, None, ALU.add))
                E.wait("scalar", "cv", tok)
                tok2 = E.op("scalar", lambda e: e.activation(
                    rrow[0:1, 0:T], rrow[0:1, 0:T], AF.Sqrt))
                E.wait("vector", "cs2", tok2)
                E.op("vector", lambda e: e.reciprocal(rrow[0:1, 0:T], rrow[0:1, 0:T]))
                tok3 = E.op("vector", lambda e: e.tensor_tensor(
                    mrow[0:1, 0:T], mrow[0:1, 0:T], rrow[0:1, 0:T], ALU.mult))
                return tok3

            bc_prev = [0]

            def bcast_row(rowap, dst_bf, tok_row):
                E.wait("tensor", "cv", tok_row)
                E.wait("tensor", "cs2", bc_prev[0])
                t1 = None
                for th in range(2):
                    t1 = E.op("tensor", lambda e, th=th: e.matmul(
                        (P[4] if th == 0 else P[5])[:, 0:HT],
                        oner[0:1, 0:128],
                        rowap[0:1, th * HT:(th + 1) * HT],
                        start=True, stop=True), inc="st")
                E.wait("scalar", "st", t1)
                t2 = None
                for th in range(2):
                    t2 = E.op("scalar", lambda e, th=th: e.copy(
                        dst_bf[:, th * HT:(th + 1) * HT],
                        (P[4] if th == 0 else P[5])[:, 0:HT]))
                bc_prev[0] = t2
                return t2

            maybe_stop("mask")
            tok_r = stats_rows(tok_a1_stats, (0, 1, 2, 3))
            pansem = ("dpan0", "dpan1", "dpan2")
            pan_used = [0, 0, 0]
            piece_ctr = [0]
            evac_pair = [0, 0]
            sqa_free = 0
            tok_attn_stats = 0
            E.wait("tensor", "cs2", tok_pall)
            E.wait("vector", "dq", tok_misc_q)

            mexp_mult_done = [0, 0]
            gen_ev_of = {}
            genev_bank = {0: 0, 1: 0, 2: 0, 3: 0}
            gen_pending = []

            def gen_group(m, g):
                pr2 = m % 2
                bk = 2 * pr2 if g % 2 == 0 else 2 * pr2 + 1
                if g == 0:
                    E.wait("tensor", "cv", evac_pair[pr2])
                    if m >= 2:
                        E.wait("tensor", "cv", mexp_mult_done[pr2])
                E.wait("tensor", "cs2", genev_bank[bk])
                tg = None
                for j in range(4):
                    kc = 4 * g + j
                    tg = E.op("tensor",
                              lambda e, kc=kc, bk=bk, j=j, m=m: e.matmul(
                                  P[bk][:, j * 128:(j + 1) * 128],
                                  pall_v[0:NBLK, kc * 128:(kc + 1) * 128],
                                  call_v[0:NBLK, m * 128:(m + 1) * 128],
                                  start=True, stop=True),
                              inc="st" if j == 3 else None) or tg
                E.wait("scalar", "st", tg)
                genev_bank[bk] = E.op(
                    "scalar", lambda e, g=g, bk=bk, m=m: e.copy(
                        mexp_s[m % 2][:, g * 512:(g + 1) * 512],
                        P[bk][:, 0:512]))
                gen_ev_of[m] = genev_bank[bk]

            def emit_gen(m):
                for g in ([] if a2_no_gen else range(NCH // 4)):
                    gen_group(m, g)

            sq_tok_of = {}

            def a2_stats(m):
                nonlocal tok_attn_stats, sqa_free
                E.wait("tensor", "cs2", sq_tok_of[m])
                for th in range(2):
                    E.op("tensor", lambda e, m=m, th=th: e.matmul(
                        P[4 + th][0:1, 0:HT], onec[:, :],
                        big2[:, m, th * HT:(th + 1) * HT],
                        start=(m == 0), stop=(m == NCH - 1)), inc="st")
                    tok_attn_stats = sqa_free = E.op(
                        "tensor", lambda e, m=m, th=th: e.matmul(
                            P[6 + th][0:1, 0:HT], onec[:, :],
                            sqst2[:, m % 2, th * HT:(th + 1) * HT],
                            start=(m == 0), stop=(m == NCH - 1)), inc="st")

            E.wait("tensor", "cv", tok_r)
            emit_gen(0)
            m0_mult_tok = [None, None]
            for h in range(2):
                sl = piece_ctr[0] % 3
                piece_ctr[0] += 1
                tok_pd = E.dma("gpsimd", pan_ap(sl), wtr[0, h, :, :],
                               pansem[sl])
                E.wait("vector", pansem[sl], tok_pd)
                E.wait("vector", "cs2", gen_ev_of[0])
                if a2_no_mult:
                    m0_mult_tok[h] = E.op("vector", lambda e, sl=sl:
                                          e.tensor_copy(tmpb[:, 0:8],
                                                        pan_ap(sl)[:, 0:8]))
                else:
                    m0_mult_tok[h] = E.op(
                        "vector", lambda e, sl=sl, h=h: e.tensor_tensor(
                            pan_ap(sl), pan_ap(sl),
                            mexp_s[0][:, h * (FH2 * 128):
                                      (h + 1) * (FH2 * 128)], ALU.mult))
            mexp_mult_done[0] = m0_mult_tok[1]

            bcast_row(mrow, bcb[:, 0, :], tok_r)
            tok_b1 = bcast_row(rrow, bcb[:, 1, :], tok_r)

            E.wait("vector", "cs2", tok_b1)
            E.wait("gpsimd", "cs2", tok_b1)
            h1_tok = {"cv": 0, "sg": 0}
            h1_ctok = [None] * NCH
            for c in range(NCH):
                eng, sem = eng_of(c)
                tb = tmpb if eng == "vector" else tmpg
                E.op(eng, lambda e, c=c, tb=tb: e.tensor_tensor(
                    tb[:, :], big1[:, c, :], bcb[:, 1, :], ALU.mult))
                h1_tok[sem] = E.op(eng, lambda e, c=c, tb=tb: e.tensor_tensor(
                    big1[:, c, :], tb[:, :], bcb[:, 0, :], ALU.subtract))
                h1_ctok[c] = (sem, h1_tok[sem])
            tok_h1 = h1_tok["cv"]
            tok_h1g = h1_tok["sg"]

            for m in range(NCH):
                maybe_stop(f"A2m{m}")
                pr = m % 2
                bA = 2 * pr
                if m == 1:
                    E.wait("tensor", "cv", tok_h1)
                    E.wait("tensor", "sg", tok_h1g)
                for h in range(2):
                    if m == 0:
                        sl = h
                        tok_mm_ready = m0_mult_tok[h]
                    else:
                        sl = piece_ctr[0] % 3
                        piece_ctr[0] += 1
                        E.wait("gpsimd", "st", pan_used[sl])
                        if sl == 2 and pan_used[2] == 0:
                            E.wait("gpsimd", "cv", tok_h1)
                            E.wait("gpsimd", "sg", tok_h1g)
                        tok_pd = E.dma("gpsimd", pan_ap(sl), wtr[m, h, :, :],
                                       pansem[sl])
                        E.wait("vector", pansem[sl], tok_pd)
                        E.wait("vector", "cs2", gen_ev_of[m])
                        if a2_no_mult:
                            tok_mm_ready = E.op("vector", lambda e, sl=sl:
                                                e.tensor_copy(tmpb[:, 0:8],
                                                              pan_ap(sl)[:, 0:8]))
                        else:
                            tok_mm_ready = E.op(
                                "vector", lambda e, sl=sl, h=h, m=m: e.tensor_tensor(
                                    pan_ap(sl), pan_ap(sl),
                                    mexp_s[m % 2][:, h * (FH2 * 128):
                                                  (h + 1) * (FH2 * 128)], ALU.mult))
                        if h == 1:
                            mexp_mult_done[pr] = tok_mm_ready
                    E.wait("tensor", "cv", tok_mm_ready)
                    if h == 0:
                        E.wait("tensor", "cv", evac_pair[pr])
                    if h == 0 and m + 1 < NCH and not a2_no_gen:
                        gen_pending.extend((m + 1, g) for g in range(NCH // 4))
                    tok_done = None
                    nmm_p = 0
                    for kcp in range(FH2):
                        if m == 0:
                            csem, ctok = h1_ctok[h * FH2 + kcp]
                            E.wait("tensor", csem, ctok)
                        for th in range(2):
                            last = (kcp == FH2 - 1 and th == 1)
                            tok_done = E.op(
                                "tensor",
                                lambda e, kcp=kcp, th=th, sl=sl, h=h, bA=bA:
                                e.matmul(
                                    P[bA + th][:, 0:HT],
                                    pan_ap(sl)[:, kcp * 128:(kcp + 1) * 128],
                                    big1[:, h * FH2 + kcp,
                                         th * HT:(th + 1) * HT],
                                    start=(h == 0 and kcp == 0),
                                    stop=(h == 1 and kcp == FH2 - 1)),
                                inc="st" if last else None) or tok_done
                            nmm_p += 1
                            if nmm_p % 8 == 0 and gen_pending:
                                gm, gg = gen_pending.pop(0)
                                gen_group(gm, gg)
                    pan_used[sl] = tok_done
                    panel_last = tok_done
                if a2_no_evac:
                    continue
                E.wait("vector", "st", panel_last)
                for th in range(2):
                    evac_pair[pr] = E.op(
                        "vector", lambda e, m=m, th=th, bA=bA: e.tensor_scalar(
                            big2[:, m, th * HT:(th + 1) * HT],
                            P[bA + th][:, 0:HT],
                            bcsb[:, m:m + 1], None, ALU.add))
                if a2_no_stats:
                    continue
                E.wait("scalar", "cv", evac_pair[pr])
                E.wait("scalar", "st", sqa_free)
                sq_tok_of[m] = E.op("scalar", lambda e, m=m: e.activation(
                    sqst2[:, m % 2, :], big2[:, m, :], AF.Square))
                if m >= 1:
                    a2_stats(m - 1)
                if m == NCH - 1:
                    a2_stats(m)

            maybe_stop("A2")
            tok_r2 = stats_rows(tok_attn_stats, (4, 5, 6, 7))
            bcast_row(mrow, bcb[:, 0, :], tok_r2)
            tok_b1 = bcast_row(rrow, bcb[:, 1, :], tok_r2)

            xssem = ("dxs0", "dxs1", "dxs2", "dxs3")
            xs_tok = [0, 0, 0, 0]
            xs_scal = [0, 0, 0, 0]
            sq_free2 = 0
            tok_x1_stats = 0
            E.wait("vector", "cs2", tok_b1)
            E.wait("sync", "st", C["st"])
            E.wait("sync", "cv", C["cv"])
            E.wait("sync", "cs2", C["cs2"])
            E.wait("gpsimd", "cs2", tok_b1)
            for c in range(NCH):
                sl = c % 4
                eng, sem = eng_of(c)
                tb = tmpb if eng == "vector" else tmpg
                E.wait("sync", "cs2", xs_scal[sl])
                E.wait("sync", xssem[sl], xs_tok[sl])
                tok_xl = E.dma("sync", xstg[:, sl, :], x_fm[c * 128:(c + 1) * 128, :],
                               xssem[sl])
                E.op(eng, lambda e, c=c, tb=tb: e.tensor_tensor(
                    tb[:, :], big2[:, c, :], bcb[:, 1, :], ALU.mult))
                E.op(eng, lambda e, c=c, tb=tb: e.tensor_tensor(
                    tb[:, :], tb[:, :], bcb[:, 0, :], ALU.subtract))
                E.wait(eng, xssem[sl], tok_xl)
                tok_x1 = E.op(eng, lambda e, c=c, sl=sl, tb=tb: e.tensor_tensor(
                    xstg[:, sl, :], xstg[:, sl, :], tb[:, :], ALU.add))
                E.wait("scalar", sem, tok_x1)
                xs_tok[sl] = E.dma("scalar", x1_dram[c * 128:(c + 1) * 128, :],
                                   xstg[:, sl, :], xssem[sl])
                xs_scal[sl] = E.op("scalar", lambda e, c=c, sl=sl: e.copy(
                    big1[:, c, :], xstg[:, sl, :]))
                E.wait("scalar", "st", sq_free2)
                tok_sq = E.op("scalar", lambda e, c=c: e.activation(
                    sqst2[:, c % 2, :], big1[:, c, :], AF.Square))
                E.wait("tensor", "cs2", tok_sq)
                for th in range(2):
                    E.op("tensor", lambda e, c=c, th=th: e.matmul(
                        P[4 + th][0:1, 0:HT], onec[:, :],
                        big1[:, c, th * HT:(th + 1) * HT],
                        start=(c == 0), stop=(c == NCH - 1)), inc="st")
                    tok_x1_stats = sq_free2 = E.op(
                        "tensor", lambda e, c=c, th=th: e.matmul(
                            P[6 + th][0:1, 0:HT], onec[:, :],
                            sqst2[:, c % 2, th * HT:(th + 1) * HT],
                            start=(c == 0), stop=(c == NCH - 1)), inc="st")
            xs_after_p1 = list(xs_tok)
            maybe_stop("A3p1")

            tok_r3 = stats_rows(tok_x1_stats, (4, 5, 6, 7))
            bcast_row(mrow, bcb[:, 0, :], tok_r3)
            tok_b1 = bcast_row(rrow, bcb[:, 1, :], tok_r3)

            sq_free3 = 0
            tok_gate = 0
            E.wait("vector", "cs2", tok_b1)
            E.wait("tensor", "dgm", tok_bmload)
            E.wait("gpsimd", "cs2", tok_b1)
            for c in range(NCH):
                eng, sem = eng_of(c)
                tb = tmpb if eng == "vector" else tmpg
                E.op(eng, lambda e, c=c, tb=tb: e.tensor_tensor(
                    tb[:, :], big1[:, c, :], bcb[:, 1, :], ALU.mult))
                tok_h2 = E.op(eng, lambda e, c=c, tb=tb: e.tensor_tensor(
                    big1[:, c, :], tb[:, :], bcb[:, 0, :], ALU.subtract))
                E.wait("scalar", "st", sq_free3)
                E.wait("scalar", sem, tok_h2)
                tok_abs = E.op("scalar", lambda e, c=c: e.activation(
                    sqst2[:, c % 2, :], big1[:, c, :], AF.Abs))
                E.wait("tensor", "cs2", tok_abs)
                for th in range(2):
                    sq_free3 = E.op("tensor", lambda e, c=c, th=th: e.matmul(
                        P[4 + th][0:1, 0:HT], onec[:, :],
                        sqst2[:, c % 2, th * HT:(th + 1) * HT],
                        start=(c == 0), stop=(c == NCH - 1)), inc="st")
                    tok_gate = E.op("tensor", lambda e, c=c, th=th: e.matmul(
                        P[6 + th][0:NE, 0:HT], gw_sb[:, c, :],
                        big1[:, c, th * HT:(th + 1) * HT],
                        start=(c == 0), stop=(c == NCH - 1)), inc="st")

            h2_last = {"cv": C["cv"], "sg": C["sg"]}
            E.wait("vector", "st", tok_gate)
            E.op("vector", lambda e: e.tensor_copy(absrow[0:1, 0:HT], P[4][0:1, 0:HT]))
            E.op("vector", lambda e: e.tensor_copy(absrow[0:1, HT:T], P[5][0:1, 0:HT]))
            E.op("vector", lambda e: e.tensor_copy(lg32[0:NE, 0:HT], P[6][0:NE, 0:HT]))
            tok_lg = E.op("vector", lambda e: e.tensor_copy(
                lg32[0:NE, HT:T], P[7][0:NE, 0:HT]))

            E.op("vector", lambda e: e.tensor_tensor(
                tmprow[0:1, 0:T], absrow[0:1, 0:T], vtrow[0:1, 0:T], ALU.mult))
            E.op("vector", lambda e: e.tensor_reduce(
                svec[0:1, 0:1], tmprow[0:1, 0:T], AX.X, ALU.add))
            E.op("vector", lambda e: e.tensor_reduce(
                svec[0:1, 1:2], absrow[0:1, 0:T], AX.X, ALU.add))
            tok_pay_vt = E.op("vector", lambda e: e.tensor_tensor(
                svec[0:1, 1:2], svec[0:1, 1:2], svec[0:1, 0:1], ALU.subtract))

            E.wait("sync", "cv", tok_lg)
            E.dma("sync", bounce[0:NE, 0:1], lg32[0:NE, 0:1], "dq", slow=True)
            E.wait("sync", "dq", C["dq"])
            tok_t0 = E.dma("sync", svec[0:1, 2:2 + NE],
                           bounce[0:NE, 0:1].rearrange("a b -> b a"), "dq", slow=True)

            E.wait("gpsimd", "cv", tok_pay_vt)
            E.wait("gpsimd", "dq", tok_t0)
            E.dma("gpsimd", cc_in.ap(), svec[0:1, 0:N_CORES], "dgm")
            E.wait("gpsimd", "dgm", C["dgm"])
            tok_cc = E.op("gpsimd", lambda e: e.collective_compute(
                "AllGather", ALU.bypass, replica_groups=[list(range(N_CORES))],
                ins=[cc_in.ap().opt()], outs=[cc_out.ap().opt()]), inc="ccs", chain=False)
            E.wait("gpsimd", "ccs", tok_cc)
            tok_gath = E.dma("gpsimd", srow[0:1, :], cc_out.ap(), "dgm")

            gcell = srow.ap()[0:1, :].rearrange("p (r c) -> p r c", c=N_CORES)
            E.wait("vector", "dgm", tok_gath)
            E.wait("vector", "dq", C["dq"])
            E.op("vector", lambda e: e.tensor_tensor(
                svec[0:1, 8:16], gcell[:, :, 0], selrow[0:1, 0:N_CORES], ALU.mult))
            E.op("vector", lambda e: e.tensor_reduce(
                svec[0:1, 37:38], svec[0:1, 8:16], AX.X, ALU.add))
            E.op("vector", lambda e: e.tensor_tensor(
                svec[0:1, 8:16], gcell[:, :, 1], selrow[0:1, 0:N_CORES], ALU.mult))
            E.op("vector", lambda e: e.tensor_reduce(
                svec[0:1, 38:39], svec[0:1, 8:16], AX.X, ALU.add))
            E.op("vector", lambda e: e.tensor_scalar(
                svec[0:1, 39:40], svec[0:1, 37:38], 1.0 / (VIS * D), 1e-8,
                ALU.mult, ALU.add))
            E.op("vector", lambda e: e.reciprocal(svec[0:1, 39:40], svec[0:1, 39:40]))
            E.op("vector", lambda e: e.tensor_scalar(
                svec[0:1, 40:41], svec[0:1, 38:39], 1.0 / ((S - VIS) * D), None,
                ALU.mult))
            E.op("vector", lambda e: e.tensor_tensor(
                svec[0:1, 40:41], svec[0:1, 40:41], svec[0:1, 39:40], ALU.mult))
            E.op("vector", lambda e: e.scalar_tensor_tensor(
                svec[0:1, 16:19], svec[0:1, 31:34], svec[0:1, 40:41],
                svec[0:1, 34:37], ALU.mult, ALU.add))
            for e_ in range(NE):
                E.op("vector", lambda e, e_=e_: e.tensor_tensor(
                    svec[0:1, 8:16], gcell[:, :, 2 + e_],
                    selrow[0:1, N_CORES:2 * N_CORES], ALU.mult))
                E.op("vector", lambda e, e_=e_: e.tensor_reduce(
                    svec[0:1, 19 + e_:20 + e_], svec[0:1, 8:16], AX.X, ALU.add))
            E.op("vector", lambda e: e.tensor_tensor(
                svec[0:1, 19:22], svec[0:1, 19:22], svec[0:1, 16:19], ALU.add))

            g0, g1, g2 = (svec[0:1, 19 + i:20 + i] for i in range(3))
            t8 = [svec[0:1, 8 + i:9 + i] for i in range(8)]

            def vv(dst, a, b, op):
                E.op("vector", lambda e, dst=dst, a=a, b=b, op=op:
                     e.tensor_tensor(dst, a, b, op))

            def one_minus(dst, a):
                E.op("vector", lambda e, dst=dst, a=a: e.tensor_scalar(
                    dst, a, -1.0, 1.0, ALU.mult, ALU.add))

            vv(t8[0], g0, g1, ALU.is_ge)
            vv(t8[1], g0, g2, ALU.is_ge)
            vv(t8[2], g1, g2, ALU.is_ge)
            vv(svec[0:1, 22:23], t8[0], t8[1], ALU.mult)
            one_minus(t8[3], t8[0])
            vv(svec[0:1, 23:24], t8[3], t8[2], ALU.mult)
            one_minus(t8[4], t8[1])
            one_minus(t8[5], t8[2])
            vv(svec[0:1, 24:25], t8[4], t8[5], ALU.mult)
            vv(t8[0], g0, g1, ALU.is_lt)
            vv(t8[1], g0, g2, ALU.is_lt)
            vv(svec[0:1, 25:26], t8[0], t8[1], ALU.mult)
            vv(t8[2], g1, g0, ALU.is_le)
            vv(t8[3], g1, g2, ALU.is_lt)
            vv(svec[0:1, 26:27], t8[2], t8[3], ALU.mult)
            vv(t8[4], g2, g0, ALU.is_le)
            vv(t8[5], g2, g1, ALU.is_le)
            vv(svec[0:1, 27:28], t8[4], t8[5], ALU.mult)
            vv(svec[0:1, 8:11], svec[0:1, 22:25], svec[0:1, 25:28], ALU.add)
            tok_oh = E.op("vector", lambda e: e.tensor_scalar(
                svec[0:1, 28:31], svec[0:1, 8:11], -1.0, 1.0, ALU.mult, ALU.add))

            E.wait("sync", "cv", tok_oh)
            E.dma("sync", bounce[0:1, 0:NE], svec[0:1, 16:19], "dq")
            E.wait("sync", "dq", C["dq"])
            tok_cc2 = E.dma("sync", corrc[0:NE, 0:1],
                            bounce[0:1, 0:NE].rearrange("a b -> b a"), "dq", slow=True)
            E.wait("vector", "dq", tok_cc2)
            tok_lgc = E.op("vector", lambda e: e.tensor_scalar(
                lg32[0:NE, :], lg32[0:NE, :], corrc[0:NE, 0:1], None, ALU.add))

            E.op("vector", lambda e: e.memset(gv.ap().rearrange("p a b -> p (a b)"),
                                              0.0))
            tok_pz = E.op("vector", lambda e: e.memset(P[0][:, 0:4 * NTC], 0.0))
            E.wait("tensor", "cv", tok_lgc)
            E.wait("tensor", "cv", tok_pz)
            t_tr = None
            for tc in range(NTC):
                t_tr = E.op("tensor", lambda e, tc=tc: e.transpose(
                    P[0][:, 4 * tc:4 * tc + NE],
                    lg32[0:NE, tc * 128:(tc + 1) * 128], idf[0:NE, 0:NE]), inc="st")
            E.wait("vector", "st", t_tr)
            E.op("vector", lambda e: e.tensor_copy(
                lgtm.ap().rearrange("p a b -> p (a b)")[:, 0:4 * NTC],
                P[0][:, 0:4 * NTC]))

            E.op("vector", lambda e: e.tensor_reduce(
                gv[:, :, 0], lgtm[:, :, 0:NE], AX.X, ALU.max))
            for e_ in range(NE):
                E.op("vector", lambda e, e_=e_: e.tensor_tensor(
                    lgtm[:, :, e_], lgtm[:, :, e_], gv[:, :, 0], ALU.subtract))
            tok_sm1 = E.op("vector", lambda e: e.tensor_copy(
                sc3.ap().rearrange("p a b -> p (a b)"),
                lgtm.ap().rearrange("p a b -> p (a b)")))
            E.wait("scalar", "cv", tok_sm1)
            tok_exp = E.op("scalar", lambda e: e.activation(
                sc3.ap().rearrange("p a b -> p (a b)"),
                sc3.ap().rearrange("p a b -> p (a b)"), AF.Exp))
            E.wait("vector", "cs2", tok_exp)
            E.op("vector", lambda e: e.tensor_reduce(
                gv[:, :, 1], sc3[:, :, 0:NE], AX.X, ALU.add))
            E.op("vector", lambda e: e.reciprocal(gv[:, :, 2], gv[:, :, 1]))
            for e_ in range(NE):
                E.op("vector", lambda e, e_=e_: e.tensor_tensor(
                    sc3[:, :, e_], sc3[:, :, e_], gv[:, :, 2], ALU.mult))
            E.op("vector", lambda e: e.tensor_reduce(
                gv[:, :, 3], sc3[:, :, 0:NE], AX.X, ALU.max))
            E.op("vector", lambda e: e.tensor_reduce(
                gv[:, :, 4], sc3[:, :, 0:NE], AX.X, ALU.min))
            E.op("vector", lambda e: e.tensor_reduce(
                gv[:, :, 5], sc3[:, :, 0:NE], AX.X, ALU.add))
            E.op("vector", lambda e: e.tensor_tensor(
                gv[:, :, 5], gv[:, :, 5], gv[:, :, 3], ALU.subtract))
            E.op("vector", lambda e: e.tensor_tensor(
                gv[:, :, 5], gv[:, :, 5], gv[:, :, 4], ALU.subtract))
            E.op("vector", lambda e: e.tensor_tensor(
                gv[:, :, 6], gv[:, :, 3], gv[:, :, 5], ALU.add))
            E.op("vector", lambda e: e.tensor_scalar(
                gv[:, :, 6], gv[:, :, 6], 1e-8, None, ALU.add))
            E.op("vector", lambda e: e.reciprocal(gv[:, :, 6], gv[:, :, 6]))
            E.op("vector", lambda e: e.tensor_tensor(
                gv[:, :, 3], gv[:, :, 3], gv[:, :, 6], ALU.mult))
            tok_tvn = E.op("vector", lambda e: e.tensor_tensor(
                gv[:, :, 5], gv[:, :, 5], gv[:, :, 6], ALU.mult))

            E.wait("tensor", "cv", tok_oh)
            E.op("tensor", lambda e: e.matmul(
                P[1][:, 0:NE], oner[0:1, 0:128], svec[0:1, 22:25],
                start=True, stop=True), inc="st")
            t_ohmm = E.op("tensor", lambda e: e.matmul(
                P[1][:, 4:4 + NE], oner[0:1, 0:128], svec[0:1, 28:31],
                start=True, stop=True), inc="st")
            E.wait("vector", "st", t_ohmm)
            tok_ohb = E.op("vector", lambda e: e.tensor_copy(
                ohb[:, 0:8], P[1][:, 0:8]))

            tok_ctm = 0
            for e_ in range(NE):
                E.op("vector", lambda e, e_=e_: e.tensor_scalar(
                    ctm[:, :, 3], gv[:, :, 3], ohb[:, e_:e_ + 1], None, ALU.mult))
                tok_ctm = E.op("vector", lambda e, e_=e_: e.scalar_tensor_tensor(
                    ctm[:, :, e_], gv[:, :, 5], ohb[:, 4 + e_:5 + e_],
                    ctm[:, :, 3], ALU.mult, ALU.add))

            crow = [mrow, rrow, tmprow]
            E.wait("tensor", "cv", tok_ctm)
            tok_coeffbc = 0
            for e_ in range(NE):
                for base in range(0, NTC, 4):
                    ks = list(range(base, min(base + 4, NTC)))
                    t_tr2 = None
                    for k, tc in enumerate(ks):
                        t_tr2 = E.op("tensor", lambda e, tc=tc, k=k, e_=e_: e.transpose(
                            P[2][0:1, 128 * k:128 * (k + 1)],
                            ctm[:, tc, e_:e_ + 1], idf[:, :]), inc="st")
                    E.wait("vector", "st", t_tr2)
                    tok_cr = E.op("vector", lambda e, e_=e_, base=base, n=len(ks):
                                  e.tensor_copy(
                                      crow[e_][0:1, base * 128:base * 128 + n * 128],
                                      P[2][0:1, 0:n * 128]))
                    E.wait("tensor", "cv", tok_cr)
                for th in range(2):
                    t_bc = E.op("tensor", lambda e, e_=e_, th=th: e.matmul(
                        P[3][:, 0:HT], oner[0:1, 0:128],
                        crow[e_][0:1, th * HT:(th + 1) * HT],
                        start=True, stop=True), inc="st")
                    E.wait("scalar", "st", t_bc)
                    tok_coeffbc = E.op("scalar", lambda e, e_=e_, th=th: e.copy(
                        bcb[:, 2 + e_, th * HT:(th + 1) * HT], P[3][:, 0:HT]))
                    E.wait("tensor", "cs2", tok_coeffbc)

            maybe_stop("gate")
            FUNCS = (AF.Tanh, AF.Gelu, AF.Silu)
            pan_used = [C["st"], C["st"], 0]
            pc3 = [0]
            pan3_init = [False]
            E.wait("tensor", "cv", C["cv"])
            E.wait("tensor", "cs2", C["cs2"])
            E.wait("scalar", "dq", tok_misc_q)
            act_ss = {}
            moe_sv = {0: 0, 1: 0}
            first_moe = True
            for _sl in range(4):
                E.wait("scalar", xssem[_sl], xs_after_p1[_sl])
            for e_ in range(NE):
                for fh in range(2):
                    act_ss.clear()
                    for i in range(FH2):
                        fp = fh * FH2 + i
                        pr = i % 2
                        for h in range(2):
                            sl = pc3[0] % 3
                            pc3[0] += 1
                            E.wait("gpsimd", "st", pan_used[sl])
                            if sl == 2 and not pan3_init[0]:
                                pan3_init[0] = True
                                E.wait("gpsimd", "cv", h2_last["cv"])
                                E.wait("gpsimd", "sg", h2_last["sg"])
                            tok_pd = E.dma("gpsimd", pan_ap(sl),
                                           w1r[e_, fp, h, :, :], pansem[sl])
                            E.wait("tensor", pansem[sl], tok_pd)
                            if h == 0:
                                if i >= 2:
                                    E.wait("tensor", "cs2", act_ss[i - 2])
                                else:
                                    E.wait("tensor", "cv", moe_sv[pr])
                            tok_done = None
                            for dcp in range(FH2):
                                for th in range(2):
                                    last = (dcp == FH2 - 1 and th == 1)
                                    tok_done = E.op(
                                        "tensor",
                                        lambda e, dcp=dcp, th=th, sl=sl, h=h, pr=pr:
                                        e.matmul(
                                            P[2 * pr + th][:, 0:HT],
                                            pan_ap(sl)[:, dcp * 128:(dcp + 1) * 128],
                                            big1[:, h * FH2 + dcp,
                                                 th * HT:(th + 1) * HT],
                                            start=(h == 0 and dcp == 0),
                                            stop=(h == 1 and dcp == FH2 - 1)),
                                        inc="st" if last else None) or tok_done
                            pan_used[sl] = tok_done
                            panel_last = tok_done
                        E.wait("scalar", "st", panel_last)
                        t_act = None
                        for th in range(2):
                            t_act = E.op(
                                "scalar",
                                lambda e, i=i, th=th, e_=e_, pr=pr, fp=fp: e.activation(
                                    act_v[:, i, th * HT:(th + 1) * HT],
                                    P[2 * pr + th][:, 0:HT], FUNCS[e_],
                                    bias=b1sb[:, e_, fp:fp + 1], scale=1.0))
                        act_ss[i] = t_act
                        E.wait("vector", "cs2", t_act)
                        act_ready = E.op("vector", lambda e, i=i, e_=e_: e.tensor_tensor(
                            act_v[:, i, :], act_v[:, i, :], bcb[:, 2 + e_, :],
                            ALU.mult))
                    last_pass = (e_ == NE - 1 and fh == 1)
                    sqc_free = 0
                    tok_moe_stats = 0
                    E.wait("tensor", "cv", act_ready)

                    def moe_stats_chunk(j):
                        nonlocal sqc_free, tok_moe_stats
                        E.wait("scalar", "cv", b2_tok[j])
                        E.wait("scalar", "st", sqc_free)
                        tok_sq = E.op("scalar", lambda e, j=j: e.activation(
                            sqst2[:, j % 2, :], big2[:, j, :], AF.Square))
                        E.wait("tensor", "cs2", tok_sq)
                        for th in range(2):
                            E.op("tensor", lambda e, j=j, th=th: e.matmul(
                                P[4 + th][0:1, 0:HT], onec[:, :],
                                big2[:, j, th * HT:(th + 1) * HT],
                                start=(j == 0), stop=(j == NCH - 1)), inc="st")
                            tok_moe_stats = sqc_free = E.op(
                                "tensor", lambda e, j=j, th=th: e.matmul(
                                    P[6 + th][0:1, 0:HT], onec[:, :],
                                    sqst2[:, j % 2, th * HT:(th + 1) * HT],
                                    start=(j == 0), stop=(j == NCH - 1)),
                                inc="st")

                    b2_tok = {}
                    for j in range(NCH):
                        pr = j % 2
                        sl = pc3[0] % 3
                        pc3[0] += 1
                        E.wait("gpsimd", "st", pan_used[sl])
                        tok_pd = E.dma("gpsimd", pan_ap(sl),
                                       w2r[e_, fh, j, :, :], pansem[sl])
                        E.wait("tensor", pansem[sl], tok_pd)
                        if j >= 2:
                            E.wait("tensor", "cv", moe_sv[pr])
                        tok_done = None
                        for fcp in range(FH2):
                            for th in range(2):
                                last = (fcp == FH2 - 1 and th == 1)
                                tok_done = E.op(
                                    "tensor",
                                    lambda e, fcp=fcp, th=th, sl=sl, pr=pr: e.matmul(
                                        P[2 * pr + th][:, 0:HT],
                                        pan_ap(sl)[:, fcp * 128:(fcp + 1) * 128],
                                        act_v[:, fcp, th * HT:(th + 1) * HT],
                                        start=(fcp == 0), stop=(fcp == FH2 - 1)),
                                    inc="st" if last else None) or tok_done
                        pan_used[sl] = tok_done
                        if last_pass and j >= 1:
                            moe_stats_chunk(j - 1)
                        E.wait("vector", "st", tok_done)
                        for th in range(2):
                            if first_moe:
                                moe_sv[pr] = E.op(
                                    "vector",
                                    lambda e, j=j, th=th, pr=pr: e.tensor_scalar(
                                        big2[:, j, th * HT:(th + 1) * HT],
                                        P[2 * pr + th][:, 0:HT],
                                        0.0, None, ALU.add))
                            else:
                                moe_sv[pr] = E.op(
                                    "vector",
                                    lambda e, j=j, th=th, pr=pr: e.tensor_tensor(
                                        big2[:, j, th * HT:(th + 1) * HT],
                                        P[2 * pr + th][:, 0:HT],
                                        big2[:, j, th * HT:(th + 1) * HT],
                                        ALU.add))
                        if last_pass:
                            for eb in range(NE):
                                b2_tok[j] = E.op(
                                    "vector",
                                    lambda e, eb=eb, j=j: e.scalar_tensor_tensor(
                                        big2[:, j, :], bcb[:, 2 + eb, :],
                                        b2sb[:, eb, j:j + 1],
                                        big2[:, j, :], ALU.mult, ALU.add))
                    first_moe = False
            moe_stats_chunk(NCH - 1)

            if debug:
                NTC_ = T // 128
                dbg_moe = nc.dram_tensor("dbg_moe", [128, NCH * T], BF16).ap()
                dbg_coeff = nc.dram_tensor("dbg_coeff", [128, 3 * T], BF16).ap()
                dbg_lg = nc.dram_tensor("dbg_lg", [NE, T], F32).ap()
                dbg_gv = nc.dram_tensor("dbg_gv", [128, NTC_ * 8], F32).ap()
                dbg_svec = nc.dram_tensor("dbg_svec", [1, 48], F32).ap()
                dbg_ctm = nc.dram_tensor("dbg_ctm", [128, NTC_ * 4], F32).ap()
                dbg_act = nc.dram_tensor("dbg_act", [128, FH2 * T], BF16).ap()
                dbg_h2 = nc.dram_tensor("dbg_h2", [128, NCH * T], BF16).ap()
                E.wait("sync", "cv", C["cv"])
                E.dma("sync", dbg_moe, big2f[:, :], "dq")
                E.dma("sync", dbg_coeff,
                      bcb.ap().rearrange("p a t -> p (a t)")[:, 2 * T:5 * T], "dq")
                E.dma("sync", dbg_lg, lg32[0:NE, :], "dq")
                E.dma("sync", dbg_gv, gv.ap().rearrange("p a b -> p (a b)"), "dq")
                E.dma("sync", dbg_svec, svec[:, :], "dq")
                E.dma("sync", dbg_ctm, ctm.ap().rearrange("p a b -> p (a b)"), "dq")
                E.dma("sync", dbg_act, ovlf[:, 0:FH2 * T], "dq")
                E.dma("sync", dbg_h2,
                      big1.ap().rearrange("p c t -> p (c t)"), "dq")

            maybe_stop("B")
            tok_r4 = stats_rows(tok_moe_stats, (4, 5, 6, 7))
            bcast_row(mrow, bcb[:, 0, :], tok_r4)
            tok_b1 = bcast_row(rrow, bcb[:, 1, :], tok_r4)

            xs_out = list(xs_after_p1)
            E.wait("vector", "cs2", tok_b1)
            E.wait("sync", "st", C["st"])
            E.wait("sync", "cv", C["cv"])
            E.wait("sync", "cs2", C["cs2"])
            E.wait("gpsimd", "cs2", tok_b1)
            for c in range(NCH):
                sl = c % 4
                eng, sem = eng_of(c)
                tb = tmpb if eng == "vector" else tmpg
                E.wait("sync", xssem[sl], xs_out[sl])
                tok_xl = E.dma("sync", xstg[:, sl, :],
                               x1_dram[c * 128:(c + 1) * 128, :], xssem[sl])
                E.op(eng, lambda e, c=c, tb=tb: e.tensor_tensor(
                    tb[:, :], big2[:, c, :], bcb[:, 1, :], ALU.mult))
                E.op(eng, lambda e, c=c, tb=tb: e.tensor_tensor(
                    tb[:, :], tb[:, :], bcb[:, 0, :], ALU.subtract))
                E.wait(eng, xssem[sl], tok_xl)
                tok_o = E.op(eng, lambda e, c=c, sl=sl, tb=tb: e.tensor_tensor(
                    xstg[:, sl, :], xstg[:, sl, :], tb[:, :], ALU.add))
                E.wait("scalar", sem, tok_o)
                xs_out[sl] = E.dma("scalar", out_fm[c * 128:(c + 1) * 128, :],
                                   xstg[:, sl, :], xssem[sl])
            for _sl in range(4):
                E.wait("sync", xssem[_sl], xs_out[_sl])


        except _StopPlan:
            dump = nc.dram_tensor("dump", [128, NCH * T], BF16).ap()
            for name in ("cv", "cs2", "sg"):
                E.wait("sync", name, E.cnt[name])
            E.wait("sync", "st", E.cnt["st"])
            E.wait("gpsimd", "cv", E.cnt["cv"])
            tok_dmp = E.dma("gpsimd", dump,
                            big1.ap().rearrange("p c t -> p (c t)"), "dgm")
            E.wait("gpsimd", "dgm", E.cnt["dgm"])
            zs = E.op("gpsimd", lambda e: e.memset(xstg[:, 0, :], 0.0))
            E.wait("sync", "sg", zs)
            tokf = [None, None]
            for c in range(NCH):
                tokf[c % 2] = E.dma("sync", out_fm[c * 128:(c + 1) * 128, :],
                                    xstg[:, 0, :], ("dxs0", "dxs1")[c % 2])
            E.wait("sync", "dxs0", tokf[0])
            E.wait("sync", "dxs1", tokf[1])

        block = st_.enter_context(nc.Block())

        @block.sync
        def _(eng):
            for f in E.q["sync"]:
                f(eng)

        @block.gpsimd
        def _(eng):
            for f in E.q["gpsimd"]:
                f(eng)

        @block.vector
        def _(eng):
            for f in E.q["vector"]:
                f(eng)

        @block.scalar
        def _(eng):
            for f in E.q["scalar"]:
                f(eng)

        @block.tensor
        def _(eng):
            for f in E.q["tensor"]:
                f(eng)

    return nc


def _prep_inputs(L0_attn_data, L0_attn_mask, L0_attn_weight, L0_attn_bias,
                 L0_ffn_gate_w, L0_ffn_gate_b, L0_ffn_w1, L0_ffn_b1,
                 L0_ffn_w2, L0_ffn_b2, D, T, S):
    NCH = D // 128
    FH2 = NCH // 2
    VIS = int(S * 0.3)
    cpb = S // T

    x = np.ascontiguousarray(np.asarray(L0_attn_data, np.float32)).reshape(-1, D)
    assert x.shape[0] == N_CORES * T

    w = np.asarray(L0_attn_weight, np.float32)
    wtr = np.ascontiguousarray(
        w.reshape(NCH, 128, 2, FH2, 128).transpose(0, 2, 4, 3, 1)
    ).reshape(NCH, 2, 128, FH2 * 128)

    w1 = np.asarray(L0_ffn_w1, np.float32)
    w1r = np.ascontiguousarray(
        w1.reshape(NE, 2, FH2, 128, NCH, 128).transpose(0, 4, 1, 3, 2, 5)
    ).reshape(NE, NCH, 2, 128, FH2 * 128)

    w2 = np.asarray(L0_ffn_w2, np.float32)
    w2r = np.ascontiguousarray(
        w2.reshape(NE, 2, FH2, 128, NCH, 128).transpose(0, 1, 4, 3, 2, 5)
    ).reshape(NE, 2, NCH, 128, FH2 * 128)

    b1 = np.asarray(L0_ffn_b1, np.float32)
    b1r = np.ascontiguousarray(b1.reshape(NE, NCH, 128).transpose(2, 0, 1))
    b2 = np.asarray(L0_ffn_b2, np.float32)
    b2cv = np.ascontiguousarray(b2.reshape(NE, NCH, 128).transpose(2, 0, 1))
    bias = np.asarray(L0_attn_bias, np.float32)
    biascv = np.ascontiguousarray(bias.reshape(NCH, 128).T)

    gw = np.asarray(L0_ffn_gate_w, np.float32)
    gwr = np.ascontiguousarray(gw[:D].reshape(NCH, 128, NE).transpose(1, 0, 2))
    gwlast = np.ascontiguousarray(gw[D:D + 1, :])
    gb = np.ascontiguousarray(np.asarray(L0_ffn_gate_b, np.float32)[None, :])

    mask = np.ascontiguousarray(np.asarray(L0_attn_mask, np.float32))

    shared = dict(wtr=wtr, w1r=w1r, w2r=w2r, b1r=b1r, b2c=b2cv,
                  biasc=biascv, gwr=gwr, gwlast=gwlast, gbias=gb)

    in_maps = []
    for c in range(N_CORES):
        shard = x[c * T:(c + 1) * T, :]
        x_fm = np.ascontiguousarray(shard.T)
        s0 = (c % cpb) * T
        spos = s0 + np.arange(T)
        vis = (spos < VIS).astype(ml_dtypes.bfloat16)[None, :]
        mys = np.zeros((1, N_CORES), np.float32)
        mys[0, (c // cpb) * cpb:(c // cpb + 1) * cpb] = 1.0
        t0s = np.zeros((1, N_CORES), np.float32)
        t0s[0, (c // cpb) * cpb] = 1.0
        m = dict(shared)
        m.update(x_fm=x_fm, visrow=np.ascontiguousarray(vis),
                 maskd=mask[c * (D // N_CORES):(c + 1) * (D // N_CORES)],
                 mysel=mys, tok0sel=t0s)
        in_maps.append(m)
    return in_maps


def run(inputs, D, T, S, nc=None):
    in_maps = _prep_inputs(D=D, T=T, S=S, **inputs)
    if nc is None:
        nc = build(D=D, T=T, S=S)
    res = run_bass_kernel_spmd(nc, in_maps, core_ids=list(range(N_CORES)))
    shards = [np.asarray(res.results[c]["out_fm"], np.float32).T
              for c in range(N_CORES)]
    return np.concatenate(shards, axis=0)


def kernel(**inputs):
    D, T, S = 4096, 1024, 4096
    out = run(inputs, D, T, S)
    B = inputs["L0_attn_data"].shape[0]
    return np.ascontiguousarray(out.reshape(B, S, D).astype(np.float32))


# version 16
# speedup vs baseline: 1.1225x; 1.0194x over previous
import os
import sys
import numpy as np
import ml_dtypes
from contextlib import ExitStack

for _p in ("/opt/trn_rl_repo",):
    if os.path.isdir(_p) and _p not in sys.path:
        sys.path.insert(0, _p)

import concourse.bass as bass
import concourse.mybir as mybir
from concourse.bass_utils import run_bass_kernel_spmd

F32 = mybir.dt.float32
BF16 = mybir.dt.bfloat16
I32 = mybir.dt.int32
AF = mybir.ActivationFunctionType
ALU = mybir.AluOpType
AX = mybir.AxisListType

N_CORES = 8
NE = 3
MBLK = 32
EPS_LN = 1e-5


class Plan:
    ENGS = ("sync", "gpsimd", "vector", "scalar", "tensor")
    CHAIN = {"vector": "cv", "scalar": "cs2", "gpsimd": "sg"}

    def __init__(self):
        self.q = {e: [] for e in self.ENGS}
        self.cnt = {}
        self.semobj = {}

    def add_sem(self, nc, stack, name):
        self.semobj[name] = stack.enter_context(nc.semaphore(name))
        self.cnt[name] = 0

    def op(self, eng, fn, inc=None, amt=1, chain=True):
        ch = self.CHAIN.get(eng) if chain else None
        if ch is not None:
            assert inc is None
            prev = self.cnt[ch]
            chsem = self.semobj[ch]
            self.cnt[ch] += 1
            tok = self.cnt[ch]
            def wrapped(e, fn=fn, chsem=chsem, prev=prev):
                if prev > 0:
                    e.wait_ge(chsem, prev)
                return fn(e).then_inc(chsem, 1)
            self.q[eng].append(wrapped)
            return tok
        if inc is None:
            self.q[eng].append(fn)
            return None
        sem = self.semobj[inc]
        self.cnt[inc] += amt
        tok = self.cnt[inc]
        self.q[eng].append(lambda e, fn=fn, sem=sem, amt=amt: fn(e).then_inc(sem, amt))
        return tok

    def wait(self, eng, name, tok):
        if tok is None or tok <= 0:
            return
        sem = self.semobj[name]
        self.q[eng].append(lambda e, sem=sem, tok=tok: e.wait_ge(sem, tok))

    def dma(self, eng, out, in_, inc, slow=False):
        if slow:
            return self.op(eng, lambda e, out=out, in_=in_: e.dma_start(
                out=out, in_=in_, allow_slow_non_contiguous=True),
                inc=inc, amt=16, chain=False)
        return self.op(eng, lambda e, out=out, in_=in_: e.dma_start(out=out, in_=in_),
                       inc=inc, amt=16, chain=False)


class _StopPlan(Exception):
    pass


def build(D=4096, T=1024, S=4096, debug=False, stop_phase=None,
          a2_no_stats=False, a2_no_evac=False, a2_no_mult=False,
          a2_no_gen=False):
    NCH = D // 128
    NBLK = D // MBLK
    HT = T // 2
    FH2 = NCH // 2
    NTC = T // 128
    VIS = int(S * 0.3)
    R32 = 128 // MBLK
    assert NBLK <= 128 and HT <= 512 and NCH % 2 == 0

    nc = bass.Bass("TRN2", target_bir_lowering=False)

    x_fm = nc.dram_tensor("x_fm", [D, T], F32, kind="ExternalInput").ap()
    wtr = nc.dram_tensor("wtr", [NCH, 2, 128, FH2 * 128], F32, kind="ExternalInput").ap()
    maskd = nc.dram_tensor("maskd", [D // N_CORES, D], F32, kind="ExternalInput").ap()
    w1r = nc.dram_tensor("w1r", [NE, NCH, 2, 128, FH2 * 128], F32, kind="ExternalInput").ap()
    w2r = nc.dram_tensor("w2r", [NE, 2, NCH, 128, FH2 * 128], F32, kind="ExternalInput").ap()
    b1r = nc.dram_tensor("b1r", [128, NE, NCH], F32, kind="ExternalInput").ap()
    b2c = nc.dram_tensor("b2c", [128, NE, NCH], F32, kind="ExternalInput").ap()
    biasc = nc.dram_tensor("biasc", [128, NCH], F32, kind="ExternalInput").ap()
    gwr = nc.dram_tensor("gwr", [128, NCH, NE], F32, kind="ExternalInput").ap()
    gwlast = nc.dram_tensor("gwlast", [1, NE], F32, kind="ExternalInput").ap()
    gbias = nc.dram_tensor("gbias", [1, NE], F32, kind="ExternalInput").ap()
    visrow = nc.dram_tensor("visrow", [1, T], BF16, kind="ExternalInput").ap()
    mysel = nc.dram_tensor("mysel", [1, N_CORES], F32, kind="ExternalInput").ap()
    tok0sel = nc.dram_tensor("tok0sel", [1, N_CORES], F32, kind="ExternalInput").ap()
    out_fm = nc.dram_tensor("out_fm", [D, T], F32, kind="ExternalOutput").ap()

    x1_dram = nc.dram_tensor("x1_dram", [D, T], F32).ap()

    cc_in = nc.dram_tensor("cc_in", [1, N_CORES], F32)
    cc_out = nc.dram_tensor("cc_out", [1, N_CORES * N_CORES], F32)
    cc2_in = nc.dram_tensor("cc2_in", [D // MBLK // N_CORES, D // MBLK], F32)
    cc2_out = nc.dram_tensor("cc2_out", [D // MBLK, D // MBLK], F32)
    bounce = nc.dram_tensor("bounce", [4, 4], F32).ap()

    E = Plan()

    with ExitStack() as st_:
        def sbt(name, shape, dtype):
            return st_.enter_context(nc.sbuf_tensor(name, shape, dtype))

        big1 = sbt("big1", [128, NCH, T], BF16)
        big2 = sbt("big2", [128, NCH, T], BF16)
        pan = sbt("pan", [128, 2, FH2 * 128], BF16)
        ovl = sbt("ovl", [128, NCH * 512], BF16)
        bcb = sbt("bcb", [128, 5, T], BF16)
        sqst2 = sbt("sqst2", [128, 2, T], BF16)
        tmpb = sbt("tmpb", [128, T], BF16)
        tmpg = sbt("tmpg", [128, T], BF16)
        gw_sb = sbt("gw_sb", [128, NCH, NE], BF16)
        onec = sbt("onec", [128, 1], BF16)
        oner = sbt("oner", [128, 128], F32)
        idf = sbt("idf", [128, 128], F32)
        rows = sbt("rows", [1, 3 * T], F32)
        absrow_t = sbt("absrow_t", [1, T], BF16)
        b1sb = sbt("b1sb", [128, NE, NCH], F32)
        b2sb = sbt("b2sb", [128, NE, NCH], F32)
        bcsb = sbt("bcsb", [128, NCH], F32)
        lgtm = sbt("lgtm", [128, NTC, 4], F32)
        sc3 = sbt("sc3", [128, NTC, 4], F32)
        ctm = sbt("ctm", [128, NTC, 4], F32)
        gv = sbt("gv", [128, NTC, 8], F32)
        ohb = sbt("ohb", [128, 8], F32)
        srow = sbt("srow", [1, N_CORES * N_CORES], F32)
        svec = sbt("svec", [1, 48], F32)
        selrow = sbt("selrow", [1, 2 * N_CORES], F32)
        vtrow = sbt("vtrow", [1, T], BF16)
        corrc = sbt("corrc", [NE, 1], F32)
        bmt = sbt("bmt", [128, 128], F32)
        bmtb = sbt("bmtb", [128, 128], BF16)
        mred = sbt("mred", [128, 1, 128], F32)

        ovlf = ovl.ap()
        bcbf = bcb.ap().rearrange("p a t -> p (a t)")

        mexp_s = [ovlf[:, 0:NCH * 128], ovlf[:, NCH * 128:2 * NCH * 128]]
        pall_v = ovlf[:, 2 * NCH * 128:2 * NCH * 128 + D]
        call_v = ovlf[:, 2 * NCH * 128 + D:2 * NCH * 128 + 2 * D]
        act_v = ovlf[:, 0:FH2 * T].rearrange("p (c t) -> p c t", t=T)
        xstg = ovlf[:, 0:T * 8].bitcast(F32).rearrange("p (a t) -> p a t", t=T)

        assert D // 2 == 2 * T
        pan3 = bcbf[:, 0:2 * T]

        def pan_ap(sl):
            return pan[:, sl, :] if sl < 2 else pan3

        big2f = big2.ap().rearrange("p c t -> p (c t)")
        NMS = 2 if NCH * T >= 4 * D else 1
        mstage = [big2f[:, (2 * i) * D:(2 * i + 2) * D].bitcast(F32)
                  for i in range(NMS)]
        lg32 = big2f[:, 0:2 * T].bitcast(F32)

        P = [st_.enter_context(nc.psum_tensor(f"P{i}", [128, 512], F32))
             for i in range(8)]

        for s in ("dxb0", "dxb1", "dxb2", "dxb3", "dmask0", "dmask1",
                  "dpan0", "dpan1", "dpan2", "dxs0", "dxs1", "dxs2", "dxs3",
                  "dq", "dgm", "ccs", "cv", "cs2", "st", "sg"):
            E.add_sem(nc, st_, s)
        C = E.cnt

        def maybe_stop(name):
            if stop_phase == name:
                raise _StopPlan()

        def eng_of(c):
            return ("gpsimd", "sg") if c % 3 == 2 else ("vector", "cv")

        mrow = rows[0:1, 0:T]
        rrow = rows[0:1, T:2 * T]
        tmprow = rows[0:1, 2 * T:3 * T]
        absrow = absrow_t[0:1, 0:T]

        try:
            E.op("gpsimd", lambda e: e.memset(onec[:, :], 1.0))
            E.op("gpsimd", lambda e: e.memset(oner[:, :], 1.0))
            tok_svz = E.op("gpsimd", lambda e: e.memset(svec[:, :], 0.0))
            E.op("gpsimd", lambda e: e.memset(idf[:, :], 0.0))
            E.op("gpsimd", lambda e: e.affine_select(
                out=idf[:, :], in_=idf[:, :], compare_op=ALU.not_equal, fill=1.0,
                base=0, pattern=[[-1, 128]], channel_multiplier=1))
            E.op("gpsimd", lambda e: e.memset(call_v, 0.0))
            tok_call = E.op("gpsimd", lambda e: e.affine_select(
                out=call_v.rearrange("p (a b) -> p a b", b=MBLK),
                in_=call_v.rearrange("p (a b) -> p a b", b=MBLK),
                compare_op=ALU.not_equal, fill=1.0, base=0,
                pattern=[[-1, D // MBLK], [0, MBLK]], channel_multiplier=1))
            tok_gw = E.dma("gpsimd", gw_sb[:, :, :], gwr[:, :, :], "dgm")
            E.wait("gpsimd", "dgm", tok_gw)
            E.dma("sync", b1sb[:, :, :], b1r[:, :, :], "dq")
            E.dma("sync", b2sb[:, :, :], b2c[:, :, :], "dq")
            E.dma("sync", bcsb[:, :], biasc[:, :], "dq")
            E.wait("sync", "sg", tok_svz)
            E.dma("sync", svec[0:1, 31:34], gwlast[:, :], "dq")
            E.dma("sync", svec[0:1, 34:37], gbias[:, :], "dq")
            E.dma("sync", selrow[0:1, 0:N_CORES], mysel[:, :], "dq")
            E.dma("sync", selrow[0:1, N_CORES:2 * N_CORES], tok0sel[:, :], "dq")
            tok_misc_q = E.dma("sync", vtrow[0:1, 0:T], visrow[:, :], "dq")

            msem = ("dmask0", "dmask1")
            mfree = [0, 0]
            tr_done = 0
            red2_done = 0
            xsem = ("dxb0", "dxb1", "dxb2", "dxb3")
            x_consumed = [0, 0, 0, 0]
            tok_xload = [None] * NCH
            sq_free = 0
            tok_a1_stats = 0
            NMC = D // 128 // N_CORES
            NIC = 4 * NMC

            def a1_chunk(c):
                nonlocal sq_free, tok_a1_stats
                sl = c % 4
                E.wait("scalar", xsem[sl], tok_xload[c])
                E.wait("scalar", "st", sq_free)
                tok_sq = E.op("scalar", lambda e, c=c: e.activation(
                    sqst2[:, c % 2, :], big1[:, c, :], AF.Square))
                E.wait("tensor", "cs2", tok_sq)
                for th in range(2):
                    E.op("tensor", lambda e, c=c, th=th: e.matmul(
                        P[0 + th][0:1, 0:HT], onec[:, :],
                        big1[:, c, th * HT:(th + 1) * HT],
                        start=(c == 0), stop=(c == NCH - 1)), inc="st")
                    tok_a1_stats = sq_free = E.op(
                        "tensor", lambda e, c=c, th=th: e.matmul(
                            P[2 + th][0:1, 0:HT], onec[:, :],
                            sqst2[:, c % 2, th * HT:(th + 1) * HT],
                            start=(c == 0), stop=(c == NCH - 1)), inc="st")
                x_consumed[sl] = tok_a1_stats

            for r in range(NMC):
                sl = r % NMS
                E.wait("sync", "cv", mfree[sl])
                tok_l = E.dma("sync", mstage[sl],
                              maskd[r * 128:(r + 1) * 128, :], msem[sl])
                E.wait("vector", msem[sl], tok_l)
                E.wait("vector", "st", tr_done)
                mfree[sl] = E.op("vector", lambda e, sl=sl: e.tensor_reduce(
                    mred[:, 0, 0:NBLK],
                    mstage[sl].rearrange("p (a b) -> p a b", b=MBLK),
                    AX.X, ALU.add))
                E.wait("tensor", "cv", mfree[sl])
                E.wait("tensor", "cv", red2_done)
                if r == 0:
                    E.wait("tensor", "sg", tok_call)
                tr_done = E.op("tensor", lambda e, sl=sl: e.transpose(
                    P[6][0:NBLK, 0:128], mred[:, 0, 0:NBLK], idf[:, :]),
                    inc="st")
                E.wait("vector", "st", tr_done)
                red2_done = E.op("vector", lambda e, r=r: e.tensor_reduce(
                    bmt[0:NBLK, R32 * r:R32 * (r + 1)],
                    P[6][0:NBLK, 0:128].rearrange("p (a b) -> p a b", b=MBLK),
                    AX.X, ALU.add))
            E.wait("tensor", "cv", red2_done)
            t_pc = E.op("tensor", lambda e: e.transpose(
                P[6][0:NIC, 0:NBLK], bmt[0:NBLK, 0:NIC], idf[0:NBLK, 0:NBLK]),
                inc="st")
            E.wait("vector", "st", t_pc)
            tok_pc = E.op("vector", lambda e: e.tensor_copy(
                mred[0:NIC, 0, 0:NBLK], P[6][0:NIC, 0:NBLK]))

            NXE = min(8, NCH - 1)
            for r in range(NCH):
                xsl = r % 4
                if r >= 4:
                    a1_chunk(r - 4)
                if r == NXE:
                    E.wait("sync", "cv", tok_pc)
                    tok_cc2s = E.dma("sync", cc2_in.ap(),
                                     mred[0:NIC, 0, 0:NBLK], "dgm")
                if r == min(20, NCH - 1):
                    E.wait("gpsimd", "dgm", tok_cc2s)
                    tok_cc2 = E.op("gpsimd", lambda e: e.collective_compute(
                        "AllGather", ALU.bypass,
                        replica_groups=[list(range(N_CORES))],
                        ins=[cc2_in.ap().opt()], outs=[cc2_out.ap().opt()]),
                        inc="ccs", chain=False)
                E.wait("gpsimd", "st", x_consumed[xsl])
                tok_xload[r] = E.dma("gpsimd", big1[:, r, :],
                                     x_fm[r * 128:(r + 1) * 128, :], xsem[xsl])
            for rr in range(max(0, NCH - 4), NCH):
                a1_chunk(rr)

            E.wait("sync", "ccs", tok_cc2)
            tok_bmload = E.dma("sync", bmt[0:NBLK, 0:NBLK], cc2_out.ap(),
                               "dgm")
            E.wait("tensor", "dgm", tok_bmload)
            t_bm = E.op("tensor", lambda e: e.transpose(
                P[6][0:NBLK, 0:NBLK], bmt[0:NBLK, 0:NBLK],
                idf[0:NBLK, 0:NBLK]), inc="st")
            E.wait("vector", "st", t_bm)
            tok_bmtb = E.op("vector", lambda e: e.tensor_scalar(
                bmtb[0:NBLK, 0:NBLK], P[6][0:NBLK, 0:NBLK], 0.0, None,
                ALU.is_gt))

            E.wait("tensor", "cv", tok_bmtb)
            tok_pall = 0
            for sp in range(D // 512):
                t_mm = E.op("tensor", lambda e, sp=sp: e.matmul(
                    P[7][0:NBLK, 0:512], bmtb[0:NBLK, 0:NBLK],
                    call_v[0:NBLK, sp * 512:(sp + 1) * 512],
                    start=True, stop=True), inc="st")
                E.wait("scalar", "st", t_mm)
                tok_pall = E.op("scalar", lambda e, sp=sp: e.copy(
                    pall_v[0:NBLK, sp * 512:(sp + 1) * 512],
                    P[7][0:NBLK, 0:512]))
                if sp < D // 512 - 1:
                    E.wait("tensor", "cs2", tok_pall)

            def stats_rows(tok_ready, banks):
                b0, b1, b2_, b3 = banks
                E.wait("vector", "st", tok_ready)
                E.op("vector", lambda e: e.tensor_scalar(
                    mrow[0:1, 0:HT], P[b0][0:1, 0:HT], 1.0 / D, None, ALU.mult))
                E.op("vector", lambda e: e.tensor_scalar(
                    mrow[0:1, HT:T], P[b1][0:1, 0:HT], 1.0 / D, None, ALU.mult))
                E.op("vector", lambda e: e.tensor_scalar(
                    rrow[0:1, 0:HT], P[b2_][0:1, 0:HT], 1.0 / D, None, ALU.mult))
                E.op("vector", lambda e: e.tensor_scalar(
                    rrow[0:1, HT:T], P[b3][0:1, 0:HT], 1.0 / D, None, ALU.mult))
                E.op("vector", lambda e: e.tensor_tensor(
                    tmprow[0:1, 0:T], mrow[0:1, 0:T], mrow[0:1, 0:T], ALU.mult))
                E.op("vector", lambda e: e.tensor_tensor(
                    rrow[0:1, 0:T], rrow[0:1, 0:T], tmprow[0:1, 0:T], ALU.subtract))
                tok = E.op("vector", lambda e: e.tensor_scalar(
                    rrow[0:1, 0:T], rrow[0:1, 0:T], EPS_LN, None, ALU.add))
                E.wait("scalar", "cv", tok)
                tok2 = E.op("scalar", lambda e: e.activation(
                    rrow[0:1, 0:T], rrow[0:1, 0:T], AF.Sqrt))
                E.wait("vector", "cs2", tok2)
                E.op("vector", lambda e: e.reciprocal(rrow[0:1, 0:T], rrow[0:1, 0:T]))
                tok3 = E.op("vector", lambda e: e.tensor_tensor(
                    mrow[0:1, 0:T], mrow[0:1, 0:T], rrow[0:1, 0:T], ALU.mult))
                return tok3

            bc_prev = [0]

            def bcast_row(rowap, dst_bf, tok_row):
                E.wait("tensor", "cv", tok_row)
                E.wait("tensor", "cs2", bc_prev[0])
                t1 = None
                for th in range(2):
                    t1 = E.op("tensor", lambda e, th=th: e.matmul(
                        (P[4] if th == 0 else P[5])[:, 0:HT],
                        oner[0:1, 0:128],
                        rowap[0:1, th * HT:(th + 1) * HT],
                        start=True, stop=True), inc="st")
                E.wait("scalar", "st", t1)
                t2 = None
                for th in range(2):
                    t2 = E.op("scalar", lambda e, th=th: e.copy(
                        dst_bf[:, th * HT:(th + 1) * HT],
                        (P[4] if th == 0 else P[5])[:, 0:HT]))
                bc_prev[0] = t2
                return t2

            maybe_stop("mask")
            tok_r = stats_rows(tok_a1_stats, (0, 1, 2, 3))
            pansem = ("dpan0", "dpan1", "dpan2")
            pan_used = [0, 0, 0]
            piece_ctr = [0]
            evac_pair = [0, 0]
            sqa_free = 0
            tok_attn_stats = 0
            E.wait("tensor", "cs2", tok_pall)
            E.wait("vector", "dq", tok_misc_q)

            mexp_mult_done = [0, 0]
            gen_ev_of = {}
            genev_bank = {0: 0, 1: 0, 2: 0, 3: 0}
            gen_pending = []

            def gen_group(m, g):
                pr2 = m % 2
                bk = 2 * pr2 if g % 2 == 0 else 2 * pr2 + 1
                if g == 0:
                    E.wait("tensor", "cv", evac_pair[pr2])
                    if m >= 2:
                        E.wait("tensor", "cv", mexp_mult_done[pr2])
                E.wait("tensor", "cs2", genev_bank[bk])
                tg = None
                for j in range(4):
                    kc = 4 * g + j
                    tg = E.op("tensor",
                              lambda e, kc=kc, bk=bk, j=j, m=m: e.matmul(
                                  P[bk][:, j * 128:(j + 1) * 128],
                                  pall_v[0:NBLK, kc * 128:(kc + 1) * 128],
                                  call_v[0:NBLK, m * 128:(m + 1) * 128],
                                  start=True, stop=True),
                              inc="st" if j == 3 else None) or tg
                E.wait("scalar", "st", tg)
                genev_bank[bk] = E.op(
                    "scalar", lambda e, g=g, bk=bk, m=m: e.copy(
                        mexp_s[m % 2][:, g * 512:(g + 1) * 512],
                        P[bk][:, 0:512]))
                gen_ev_of[m] = genev_bank[bk]

            def emit_gen(m):
                for g in ([] if a2_no_gen else range(NCH // 4)):
                    gen_group(m, g)

            sq_tok_of = {}

            def a2_stats(m):
                nonlocal tok_attn_stats, sqa_free
                E.wait("tensor", "cs2", sq_tok_of[m])
                for th in range(2):
                    E.op("tensor", lambda e, m=m, th=th: e.matmul(
                        P[4 + th][0:1, 0:HT], onec[:, :],
                        big2[:, m, th * HT:(th + 1) * HT],
                        start=(m == 0), stop=(m == NCH - 1)), inc="st")
                    tok_attn_stats = sqa_free = E.op(
                        "tensor", lambda e, m=m, th=th: e.matmul(
                            P[6 + th][0:1, 0:HT], onec[:, :],
                            sqst2[:, m % 2, th * HT:(th + 1) * HT],
                            start=(m == 0), stop=(m == NCH - 1)), inc="st")

            E.wait("tensor", "cv", tok_r)
            emit_gen(0)
            m0_mult_tok = [None, None]
            for h in range(2):
                sl = piece_ctr[0] % 3
                piece_ctr[0] += 1
                tok_pd = E.dma("gpsimd", pan_ap(sl), wtr[0, h, :, :],
                               pansem[sl])
                E.wait("vector", pansem[sl], tok_pd)
                E.wait("vector", "cs2", gen_ev_of[0])
                if a2_no_mult:
                    m0_mult_tok[h] = E.op("vector", lambda e, sl=sl:
                                          e.tensor_copy(tmpb[:, 0:8],
                                                        pan_ap(sl)[:, 0:8]))
                else:
                    m0_mult_tok[h] = E.op(
                        "vector", lambda e, sl=sl, h=h: e.tensor_tensor(
                            pan_ap(sl), pan_ap(sl),
                            mexp_s[0][:, h * (FH2 * 128):
                                      (h + 1) * (FH2 * 128)], ALU.mult))
            mexp_mult_done[0] = m0_mult_tok[1]

            bcast_row(mrow, bcb[:, 0, :], tok_r)
            tok_b1 = bcast_row(rrow, bcb[:, 1, :], tok_r)

            E.wait("vector", "cs2", tok_b1)
            E.wait("gpsimd", "cs2", tok_b1)
            h1_tok = {"cv": 0, "sg": 0}
            h1_ctok = [None] * NCH
            for c in range(NCH):
                eng, sem = eng_of(c)
                tb = tmpb if eng == "vector" else tmpg
                E.op(eng, lambda e, c=c, tb=tb: e.tensor_tensor(
                    tb[:, :], big1[:, c, :], bcb[:, 1, :], ALU.mult))
                h1_tok[sem] = E.op(eng, lambda e, c=c, tb=tb: e.tensor_tensor(
                    big1[:, c, :], tb[:, :], bcb[:, 0, :], ALU.subtract))
                h1_ctok[c] = (sem, h1_tok[sem])
            tok_h1 = h1_tok["cv"]
            tok_h1g = h1_tok["sg"]

            for m in range(NCH):
                maybe_stop(f"A2m{m}")
                pr = m % 2
                bA = 2 * pr
                if m == 1:
                    E.wait("tensor", "cv", tok_h1)
                    E.wait("tensor", "sg", tok_h1g)
                for h in range(2):
                    if m == 0:
                        sl = h
                        tok_mm_ready = m0_mult_tok[h]
                    else:
                        sl = piece_ctr[0] % 3
                        piece_ctr[0] += 1
                        E.wait("gpsimd", "st", pan_used[sl])
                        if sl == 2 and pan_used[2] == 0:
                            E.wait("gpsimd", "cv", tok_h1)
                            E.wait("gpsimd", "sg", tok_h1g)
                        tok_pd = E.dma("gpsimd", pan_ap(sl), wtr[m, h, :, :],
                                       pansem[sl])
                        E.wait("vector", pansem[sl], tok_pd)
                        E.wait("vector", "cs2", gen_ev_of[m])
                        if a2_no_mult:
                            tok_mm_ready = E.op("vector", lambda e, sl=sl:
                                                e.tensor_copy(tmpb[:, 0:8],
                                                              pan_ap(sl)[:, 0:8]))
                        else:
                            tok_mm_ready = E.op(
                                "vector", lambda e, sl=sl, h=h, m=m: e.tensor_tensor(
                                    pan_ap(sl), pan_ap(sl),
                                    mexp_s[m % 2][:, h * (FH2 * 128):
                                                  (h + 1) * (FH2 * 128)], ALU.mult))
                        if h == 1:
                            mexp_mult_done[pr] = tok_mm_ready
                    E.wait("tensor", "cv", tok_mm_ready)
                    if h == 0:
                        E.wait("tensor", "cv", evac_pair[pr])
                    if h == 0 and m + 1 < NCH and not a2_no_gen:
                        gen_pending.extend((m + 1, g) for g in range(NCH // 4))
                    tok_done = None
                    nmm_p = 0
                    for kcp in range(FH2):
                        if m == 0:
                            csem, ctok = h1_ctok[h * FH2 + kcp]
                            E.wait("tensor", csem, ctok)
                        for th in range(2):
                            last = (kcp == FH2 - 1 and th == 1)
                            tok_done = E.op(
                                "tensor",
                                lambda e, kcp=kcp, th=th, sl=sl, h=h, bA=bA:
                                e.matmul(
                                    P[bA + th][:, 0:HT],
                                    pan_ap(sl)[:, kcp * 128:(kcp + 1) * 128],
                                    big1[:, h * FH2 + kcp,
                                         th * HT:(th + 1) * HT],
                                    start=(h == 0 and kcp == 0),
                                    stop=(h == 1 and kcp == FH2 - 1)),
                                inc="st" if last else None) or tok_done
                            nmm_p += 1
                            if nmm_p % 8 == 0 and gen_pending:
                                gm, gg = gen_pending.pop(0)
                                gen_group(gm, gg)
                    pan_used[sl] = tok_done
                    panel_last = tok_done
                if a2_no_evac:
                    continue
                E.wait("vector", "st", panel_last)
                for th in range(2):
                    evac_pair[pr] = E.op(
                        "vector", lambda e, m=m, th=th, bA=bA: e.tensor_scalar(
                            big2[:, m, th * HT:(th + 1) * HT],
                            P[bA + th][:, 0:HT],
                            bcsb[:, m:m + 1], None, ALU.add))
                if a2_no_stats:
                    continue
                E.wait("scalar", "cv", evac_pair[pr])
                E.wait("scalar", "st", sqa_free)
                sq_tok_of[m] = E.op("scalar", lambda e, m=m: e.activation(
                    sqst2[:, m % 2, :], big2[:, m, :], AF.Square))
                if m >= 1:
                    a2_stats(m - 1)
                if m == NCH - 1:
                    a2_stats(m)

            maybe_stop("A2")
            b_pan_tok = [None, None]
            for h in range(2):
                E.wait("gpsimd", "st", pan_used[h])
                b_pan_tok[h] = E.dma("gpsimd", pan_ap(h), w1r[0, 0, h, :, :],
                                     pansem[h])
            a2_evac_pair = list(evac_pair)
            a2_last_genev = gen_ev_of[NCH - 1]
            tok_r2 = stats_rows(tok_attn_stats, (4, 5, 6, 7))
            bcast_row(mrow, bcb[:, 0, :], tok_r2)
            tok_b1 = bcast_row(rrow, bcb[:, 1, :], tok_r2)

            xssem = ("dxs0", "dxs1", "dxs2", "dxs3")
            xs_tok = [0, 0, 0, 0]
            xs_scal = [0, 0, 0, 0]
            sq_free2 = 0
            tok_x1_stats = 0
            E.wait("vector", "cs2", tok_b1)
            E.wait("sync", "st", C["st"])
            E.wait("sync", "cv", C["cv"])
            E.wait("sync", "cs2", C["cs2"])
            E.wait("gpsimd", "cs2", tok_b1)
            for c in range(NCH):
                sl = c % 4
                eng, sem = eng_of(c)
                tb = tmpb if eng == "vector" else tmpg
                E.wait("sync", "cs2", xs_scal[sl])
                E.wait("sync", xssem[sl], xs_tok[sl])
                tok_xl = E.dma("sync", xstg[:, sl, :], x_fm[c * 128:(c + 1) * 128, :],
                               xssem[sl])
                E.op(eng, lambda e, c=c, tb=tb: e.tensor_tensor(
                    tb[:, :], big2[:, c, :], bcb[:, 1, :], ALU.mult))
                E.op(eng, lambda e, c=c, tb=tb: e.tensor_tensor(
                    tb[:, :], tb[:, :], bcb[:, 0, :], ALU.subtract))
                E.wait(eng, xssem[sl], tok_xl)
                tok_x1 = E.op(eng, lambda e, c=c, sl=sl, tb=tb: e.tensor_tensor(
                    xstg[:, sl, :], xstg[:, sl, :], tb[:, :], ALU.add))
                E.wait("scalar", sem, tok_x1)
                xs_tok[sl] = E.dma("scalar", x1_dram[c * 128:(c + 1) * 128, :],
                                   xstg[:, sl, :], xssem[sl])
                xs_scal[sl] = E.op("scalar", lambda e, c=c, sl=sl: e.copy(
                    big1[:, c, :], xstg[:, sl, :]))
                E.wait("scalar", "st", sq_free2)
                tok_sq = E.op("scalar", lambda e, c=c: e.activation(
                    sqst2[:, c % 2, :], big1[:, c, :], AF.Square))
                E.wait("tensor", "cs2", tok_sq)
                for th in range(2):
                    E.op("tensor", lambda e, c=c, th=th: e.matmul(
                        P[4 + th][0:1, 0:HT], onec[:, :],
                        big1[:, c, th * HT:(th + 1) * HT],
                        start=(c == 0), stop=(c == NCH - 1)), inc="st")
                    tok_x1_stats = sq_free2 = E.op(
                        "tensor", lambda e, c=c, th=th: e.matmul(
                            P[6 + th][0:1, 0:HT], onec[:, :],
                            sqst2[:, c % 2, th * HT:(th + 1) * HT],
                            start=(c == 0), stop=(c == NCH - 1)), inc="st")
            xs_after_p1 = list(xs_tok)
            maybe_stop("A3p1")

            tok_r3 = stats_rows(tok_x1_stats, (4, 5, 6, 7))
            bcast_row(mrow, bcb[:, 0, :], tok_r3)
            tok_b1 = bcast_row(rrow, bcb[:, 1, :], tok_r3)

            sq_free3 = 0
            tok_gate = 0
            h2_ctok = [None] * NCH
            E.wait("vector", "cs2", tok_b1)
            E.wait("tensor", "dgm", tok_bmload)
            E.wait("gpsimd", "cs2", tok_b1)
            for c in range(NCH):
                eng, sem = eng_of(c)
                tb = tmpb if eng == "vector" else tmpg
                E.op(eng, lambda e, c=c, tb=tb: e.tensor_tensor(
                    tb[:, :], big1[:, c, :], bcb[:, 1, :], ALU.mult))
                tok_h2 = E.op(eng, lambda e, c=c, tb=tb: e.tensor_tensor(
                    big1[:, c, :], tb[:, :], bcb[:, 0, :], ALU.subtract))
                h2_ctok[c] = (sem, tok_h2)
                E.wait("scalar", "st", sq_free3)
                E.wait("scalar", sem, tok_h2)
                tok_abs = E.op("scalar", lambda e, c=c: e.activation(
                    sqst2[:, c % 2, :], big1[:, c, :], AF.Abs))
                E.wait("tensor", "cs2", tok_abs)
                for th in range(2):
                    sq_free3 = E.op("tensor", lambda e, c=c, th=th: e.matmul(
                        P[4 + th][0:1, 0:HT], onec[:, :],
                        sqst2[:, c % 2, th * HT:(th + 1) * HT],
                        start=(c == 0), stop=(c == NCH - 1)), inc="st")
                    tok_gate = E.op("tensor", lambda e, c=c, th=th: e.matmul(
                        P[6 + th][0:NE, 0:HT], gw_sb[:, c, :],
                        big1[:, c, th * HT:(th + 1) * HT],
                        start=(c == 0), stop=(c == NCH - 1)), inc="st")

            h2_last = {"cv": C["cv"], "sg": C["sg"]}
            E.wait("vector", "st", tok_gate)
            E.op("vector", lambda e: e.tensor_copy(absrow[0:1, 0:HT], P[4][0:1, 0:HT]))
            E.op("vector", lambda e: e.tensor_copy(absrow[0:1, HT:T], P[5][0:1, 0:HT]))
            E.op("vector", lambda e: e.tensor_copy(lg32[0:NE, 0:HT], P[6][0:NE, 0:HT]))
            tok_lg = E.op("vector", lambda e: e.tensor_copy(
                lg32[0:NE, HT:T], P[7][0:NE, 0:HT]))

            E.op("vector", lambda e: e.tensor_tensor(
                tmprow[0:1, 0:T], absrow[0:1, 0:T], vtrow[0:1, 0:T], ALU.mult))
            E.op("vector", lambda e: e.tensor_reduce(
                svec[0:1, 0:1], tmprow[0:1, 0:T], AX.X, ALU.add))
            E.op("vector", lambda e: e.tensor_reduce(
                svec[0:1, 1:2], absrow[0:1, 0:T], AX.X, ALU.add))
            tok_pay_vt = E.op("vector", lambda e: e.tensor_tensor(
                svec[0:1, 1:2], svec[0:1, 1:2], svec[0:1, 0:1], ALU.subtract))

            E.wait("sync", "cv", tok_lg)
            E.dma("sync", bounce[0:NE, 0:1], lg32[0:NE, 0:1], "dq", slow=True)
            E.wait("sync", "dq", C["dq"])
            tok_t0 = E.dma("sync", svec[0:1, 2:2 + NE],
                           bounce[0:NE, 0:1].rearrange("a b -> b a"), "dq", slow=True)

            E.wait("sync", "cv", tok_pay_vt)
            E.wait("sync", "dq", tok_t0)
            tok_ccin = E.dma("sync", cc_in.ap(), svec[0:1, 0:N_CORES], "dgm")
            E.wait("gpsimd", "dgm", tok_ccin)
            tok_cc = E.op("gpsimd", lambda e: e.collective_compute(
                "AllGather", ALU.bypass, replica_groups=[list(range(N_CORES))],
                ins=[cc_in.ap().opt()], outs=[cc_out.ap().opt()]), inc="ccs", chain=False)
            E.wait("sync", "ccs", tok_cc)
            tok_gath = E.dma("sync", srow[0:1, :], cc_out.ap(), "dgm")

            def build_gate_tail():
                gcell = srow.ap()[0:1, :].rearrange("p (r c) -> p r c", c=N_CORES)
                E.wait("vector", "dgm", tok_gath)
                E.wait("vector", "dq", C["dq"])
                E.op("vector", lambda e: e.tensor_tensor(
                    svec[0:1, 8:16], gcell[:, :, 0], selrow[0:1, 0:N_CORES], ALU.mult))
                E.op("vector", lambda e: e.tensor_reduce(
                    svec[0:1, 37:38], svec[0:1, 8:16], AX.X, ALU.add))
                E.op("vector", lambda e: e.tensor_tensor(
                    svec[0:1, 8:16], gcell[:, :, 1], selrow[0:1, 0:N_CORES], ALU.mult))
                E.op("vector", lambda e: e.tensor_reduce(
                    svec[0:1, 38:39], svec[0:1, 8:16], AX.X, ALU.add))
                E.op("vector", lambda e: e.tensor_scalar(
                    svec[0:1, 39:40], svec[0:1, 37:38], 1.0 / (VIS * D), 1e-8,
                    ALU.mult, ALU.add))
                E.op("vector", lambda e: e.reciprocal(svec[0:1, 39:40], svec[0:1, 39:40]))
                E.op("vector", lambda e: e.tensor_scalar(
                    svec[0:1, 40:41], svec[0:1, 38:39], 1.0 / ((S - VIS) * D), None,
                    ALU.mult))
                E.op("vector", lambda e: e.tensor_tensor(
                    svec[0:1, 40:41], svec[0:1, 40:41], svec[0:1, 39:40], ALU.mult))
                E.op("vector", lambda e: e.scalar_tensor_tensor(
                    svec[0:1, 16:19], svec[0:1, 31:34], svec[0:1, 40:41],
                    svec[0:1, 34:37], ALU.mult, ALU.add))
                for e_ in range(NE):
                    E.op("vector", lambda e, e_=e_: e.tensor_tensor(
                        svec[0:1, 8:16], gcell[:, :, 2 + e_],
                        selrow[0:1, N_CORES:2 * N_CORES], ALU.mult))
                    E.op("vector", lambda e, e_=e_: e.tensor_reduce(
                        svec[0:1, 19 + e_:20 + e_], svec[0:1, 8:16], AX.X, ALU.add))
                E.op("vector", lambda e: e.tensor_tensor(
                    svec[0:1, 19:22], svec[0:1, 19:22], svec[0:1, 16:19], ALU.add))

                g0, g1, g2 = (svec[0:1, 19 + i:20 + i] for i in range(3))
                t8 = [svec[0:1, 8 + i:9 + i] for i in range(8)]

                def vv(dst, a, b, op):
                    E.op("vector", lambda e, dst=dst, a=a, b=b, op=op:
                         e.tensor_tensor(dst, a, b, op))

                def one_minus(dst, a):
                    E.op("vector", lambda e, dst=dst, a=a: e.tensor_scalar(
                        dst, a, -1.0, 1.0, ALU.mult, ALU.add))

                vv(t8[0], g0, g1, ALU.is_ge)
                vv(t8[1], g0, g2, ALU.is_ge)
                vv(t8[2], g1, g2, ALU.is_ge)
                vv(svec[0:1, 22:23], t8[0], t8[1], ALU.mult)
                one_minus(t8[3], t8[0])
                vv(svec[0:1, 23:24], t8[3], t8[2], ALU.mult)
                one_minus(t8[4], t8[1])
                one_minus(t8[5], t8[2])
                vv(svec[0:1, 24:25], t8[4], t8[5], ALU.mult)
                vv(t8[0], g0, g1, ALU.is_lt)
                vv(t8[1], g0, g2, ALU.is_lt)
                vv(svec[0:1, 25:26], t8[0], t8[1], ALU.mult)
                vv(t8[2], g1, g0, ALU.is_le)
                vv(t8[3], g1, g2, ALU.is_lt)
                vv(svec[0:1, 26:27], t8[2], t8[3], ALU.mult)
                vv(t8[4], g2, g0, ALU.is_le)
                vv(t8[5], g2, g1, ALU.is_le)
                vv(svec[0:1, 27:28], t8[4], t8[5], ALU.mult)
                vv(svec[0:1, 8:11], svec[0:1, 22:25], svec[0:1, 25:28], ALU.add)
                tok_oh = E.op("vector", lambda e: e.tensor_scalar(
                    svec[0:1, 28:31], svec[0:1, 8:11], -1.0, 1.0, ALU.mult, ALU.add))

                E.wait("sync", "cv", tok_oh)
                E.dma("sync", bounce[0:1, 0:NE], svec[0:1, 16:19], "dq")
                E.wait("sync", "dq", C["dq"])
                tok_cc2 = E.dma("sync", corrc[0:NE, 0:1],
                                bounce[0:1, 0:NE].rearrange("a b -> b a"), "dq", slow=True)
                E.wait("vector", "dq", tok_cc2)
                tok_lgc = E.op("vector", lambda e: e.tensor_scalar(
                    lg32[0:NE, :], lg32[0:NE, :], corrc[0:NE, 0:1], None, ALU.add))

                E.op("vector", lambda e: e.memset(gv.ap().rearrange("p a b -> p (a b)"),
                                                  0.0))
                tok_pz = E.op("vector", lambda e: e.memset(P[6][:, 0:4 * NTC], 0.0))
                E.wait("tensor", "cv", tok_lgc)
                E.wait("tensor", "cv", tok_pz)
                t_tr = None
                for tc in range(NTC):
                    t_tr = E.op("tensor", lambda e, tc=tc: e.transpose(
                        P[6][:, 4 * tc:4 * tc + NE],
                        lg32[0:NE, tc * 128:(tc + 1) * 128], idf[0:NE, 0:NE]), inc="st")
                E.wait("vector", "st", t_tr)
                E.op("vector", lambda e: e.tensor_copy(
                    lgtm.ap().rearrange("p a b -> p (a b)")[:, 0:4 * NTC],
                    P[6][:, 0:4 * NTC]))

                E.op("vector", lambda e: e.tensor_reduce(
                    gv[:, :, 0], lgtm[:, :, 0:NE], AX.X, ALU.max))
                for e_ in range(NE):
                    E.op("vector", lambda e, e_=e_: e.tensor_tensor(
                        lgtm[:, :, e_], lgtm[:, :, e_], gv[:, :, 0], ALU.subtract))
                tok_sm1 = E.op("vector", lambda e: e.tensor_copy(
                    sc3.ap().rearrange("p a b -> p (a b)"),
                    lgtm.ap().rearrange("p a b -> p (a b)")))
                E.wait("scalar", "cv", tok_sm1)
                tok_exp = E.op("scalar", lambda e: e.activation(
                    sc3.ap().rearrange("p a b -> p (a b)"),
                    sc3.ap().rearrange("p a b -> p (a b)"), AF.Exp))
                E.wait("vector", "cs2", tok_exp)
                E.op("vector", lambda e: e.tensor_reduce(
                    gv[:, :, 1], sc3[:, :, 0:NE], AX.X, ALU.add))
                E.op("vector", lambda e: e.reciprocal(gv[:, :, 2], gv[:, :, 1]))
                for e_ in range(NE):
                    E.op("vector", lambda e, e_=e_: e.tensor_tensor(
                        sc3[:, :, e_], sc3[:, :, e_], gv[:, :, 2], ALU.mult))
                E.op("vector", lambda e: e.tensor_reduce(
                    gv[:, :, 3], sc3[:, :, 0:NE], AX.X, ALU.max))
                E.op("vector", lambda e: e.tensor_reduce(
                    gv[:, :, 4], sc3[:, :, 0:NE], AX.X, ALU.min))
                E.op("vector", lambda e: e.tensor_reduce(
                    gv[:, :, 5], sc3[:, :, 0:NE], AX.X, ALU.add))
                E.op("vector", lambda e: e.tensor_tensor(
                    gv[:, :, 5], gv[:, :, 5], gv[:, :, 3], ALU.subtract))
                E.op("vector", lambda e: e.tensor_tensor(
                    gv[:, :, 5], gv[:, :, 5], gv[:, :, 4], ALU.subtract))
                E.op("vector", lambda e: e.tensor_tensor(
                    gv[:, :, 6], gv[:, :, 3], gv[:, :, 5], ALU.add))
                E.op("vector", lambda e: e.tensor_scalar(
                    gv[:, :, 6], gv[:, :, 6], 1e-8, None, ALU.add))
                E.op("vector", lambda e: e.reciprocal(gv[:, :, 6], gv[:, :, 6]))
                E.op("vector", lambda e: e.tensor_tensor(
                    gv[:, :, 3], gv[:, :, 3], gv[:, :, 6], ALU.mult))
                tok_tvn = E.op("vector", lambda e: e.tensor_tensor(
                    gv[:, :, 5], gv[:, :, 5], gv[:, :, 6], ALU.mult))

                E.wait("tensor", "cv", tok_oh)
                E.op("tensor", lambda e: e.matmul(
                    P[7][:, 0:NE], oner[0:1, 0:128], svec[0:1, 22:25],
                    start=True, stop=True), inc="st")
                t_ohmm = E.op("tensor", lambda e: e.matmul(
                    P[7][:, 4:4 + NE], oner[0:1, 0:128], svec[0:1, 28:31],
                    start=True, stop=True), inc="st")
                E.wait("vector", "st", t_ohmm)
                tok_ohb = E.op("vector", lambda e: e.tensor_copy(
                    ohb[:, 0:8], P[7][:, 0:8]))

                tok_ctm = 0
                for e_ in range(NE):
                    E.op("vector", lambda e, e_=e_: e.tensor_scalar(
                        ctm[:, :, 3], gv[:, :, 3], ohb[:, e_:e_ + 1], None, ALU.mult))
                    tok_ctm = E.op("vector", lambda e, e_=e_: e.scalar_tensor_tensor(
                        ctm[:, :, e_], gv[:, :, 5], ohb[:, 4 + e_:5 + e_],
                        ctm[:, :, 3], ALU.mult, ALU.add))

                crow = [mrow, rrow, tmprow]
                E.wait("tensor", "cv", tok_ctm)
                tok_coeffbc = 0
                for e_ in range(NE):
                    for base in range(0, NTC, 4):
                        ks = list(range(base, min(base + 4, NTC)))
                        t_tr2 = None
                        for k, tc in enumerate(ks):
                            t_tr2 = E.op("tensor", lambda e, tc=tc, k=k, e_=e_: e.transpose(
                                P[6][0:1, 128 * k:128 * (k + 1)],
                                ctm[:, tc, e_:e_ + 1], idf[:, :]), inc="st")
                        E.wait("vector", "st", t_tr2)
                        tok_cr = E.op("vector", lambda e, e_=e_, base=base, n=len(ks):
                                      e.tensor_copy(
                                          crow[e_][0:1, base * 128:base * 128 + n * 128],
                                          P[6][0:1, 0:n * 128]))
                        E.wait("tensor", "cv", tok_cr)
                    for th in range(2):
                        t_bc = E.op("tensor", lambda e, e_=e_, th=th: e.matmul(
                            P[7][:, 0:HT], oner[0:1, 0:128],
                            crow[e_][0:1, th * HT:(th + 1) * HT],
                            start=True, stop=True), inc="st")
                        E.wait("scalar", "st", t_bc)
                        tok_coeffbc = E.op("scalar", lambda e, e_=e_, th=th: e.copy(
                            bcb[:, 2 + e_, th * HT:(th + 1) * HT], P[7][:, 0:HT]))
                        E.wait("tensor", "cs2", tok_coeffbc)

                return tok_coeffbc, tok_cr

            maybe_stop("gate")
            FUNCS = (AF.Tanh, AF.Gelu, AF.Silu)
            pan_used = [0, 0, 0]
            pc3 = [0]
            pan3_init = [False]
            E.wait("tensor", "cs2", a2_last_genev)
            E.wait("scalar", "dq", tok_misc_q)
            act_ss = {}
            moe_sv = {0: a2_evac_pair[0], 1: a2_evac_pair[1]}
            first_moe = True
            for _sl in range(4):
                E.wait("scalar", xssem[_sl], xs_after_p1[_sl])
            for e_ in range(NE):
                for fh in range(2):
                    act_ss.clear()
                    for i in range(FH2):
                        fp = fh * FH2 + i
                        pr = i % 2
                        for h in range(2):
                            if e_ == 0 and fh == 0 and i == 0:
                                sl = h
                                pc3[0] += 1
                                tok_pd = b_pan_tok[h]
                            else:
                                sl = pc3[0] % 3
                                pc3[0] += 1
                                E.wait("gpsimd", "st", pan_used[sl])
                                if sl == 2 and not pan3_init[0]:
                                    pan3_init[0] = True
                                    E.wait("gpsimd", "cv", h2_last["cv"])
                                    E.wait("gpsimd", "sg", h2_last["sg"])
                                tok_pd = E.dma("gpsimd", pan_ap(sl),
                                               w1r[e_, fp, h, :, :], pansem[sl])
                            E.wait("tensor", pansem[sl], tok_pd)
                            if h == 0:
                                if i >= 2:
                                    E.wait("tensor", "cs2", act_ss[i - 2])
                                else:
                                    E.wait("tensor", "cv", moe_sv[pr])
                            tok_done = None
                            for dcp in range(FH2):
                                if e_ == 0 and fh == 0 and i == 0:
                                    csem, ctok = h2_ctok[h * FH2 + dcp]
                                    E.wait("tensor", csem, ctok)
                                for th in range(2):
                                    last = (dcp == FH2 - 1 and th == 1)
                                    tok_done = E.op(
                                        "tensor",
                                        lambda e, dcp=dcp, th=th, sl=sl, h=h, pr=pr:
                                        e.matmul(
                                            P[2 * pr + th][:, 0:HT],
                                            pan_ap(sl)[:, dcp * 128:(dcp + 1) * 128],
                                            big1[:, h * FH2 + dcp,
                                                 th * HT:(th + 1) * HT],
                                            start=(h == 0 and dcp == 0),
                                            stop=(h == 1 and dcp == FH2 - 1)),
                                        inc="st" if last else None) or tok_done
                            pan_used[sl] = tok_done
                            panel_last = tok_done
                        E.wait("scalar", "st", panel_last)
                        t_act = None
                        for th in range(2):
                            t_act = E.op(
                                "scalar",
                                lambda e, i=i, th=th, e_=e_, pr=pr, fp=fp: e.activation(
                                    act_v[:, i, th * HT:(th + 1) * HT],
                                    P[2 * pr + th][:, 0:HT], FUNCS[e_],
                                    bias=b1sb[:, e_, fp:fp + 1], scale=1.0))
                        act_ss[i] = t_act
                        if not (e_ == 0 and fh == 0):
                            E.wait("vector", "cs2", t_act)
                            act_ready = E.op(
                                "vector", lambda e, i=i, e_=e_: e.tensor_tensor(
                                    act_v[:, i, :], act_v[:, i, :],
                                    bcb[:, 2 + e_, :], ALU.mult))
                    if e_ == 0 and fh == 0:
                        tok_coeffbc, gate_cv_last = build_gate_tail()
                        E.wait("vector", "cs2", tok_coeffbc)
                        for i2 in range(FH2):
                            act_ready = E.op(
                                "vector", lambda e, i2=i2: e.tensor_tensor(
                                    act_v[:, i2, :], act_v[:, i2, :],
                                    bcb[:, 2, :], ALU.mult))
                    last_pass = (e_ == NE - 1 and fh == 1)
                    sqc_free = 0
                    tok_moe_stats = 0
                    E.wait("tensor", "cv", act_ready)
                    if last_pass:
                        E.wait("tensor", "cv", gate_cv_last)
                        E.wait("tensor", "cs2", tok_coeffbc)

                    def moe_stats_chunk(j):
                        nonlocal sqc_free, tok_moe_stats
                        E.wait("scalar", "cv", b2_tok[j])
                        E.wait("scalar", "st", sqc_free)
                        tok_sq = E.op("scalar", lambda e, j=j: e.activation(
                            sqst2[:, j % 2, :], big2[:, j, :], AF.Square))
                        E.wait("tensor", "cs2", tok_sq)
                        for th in range(2):
                            E.op("tensor", lambda e, j=j, th=th: e.matmul(
                                P[4 + th][0:1, 0:HT], onec[:, :],
                                big2[:, j, th * HT:(th + 1) * HT],
                                start=(j == 0), stop=(j == NCH - 1)), inc="st")
                            tok_moe_stats = sqc_free = E.op(
                                "tensor", lambda e, j=j, th=th: e.matmul(
                                    P[6 + th][0:1, 0:HT], onec[:, :],
                                    sqst2[:, j % 2, th * HT:(th + 1) * HT],
                                    start=(j == 0), stop=(j == NCH - 1)),
                                inc="st")

                    b2_tok = {}
                    for j in range(NCH):
                        pr = j % 2
                        sl = pc3[0] % 3
                        pc3[0] += 1
                        E.wait("gpsimd", "st", pan_used[sl])
                        tok_pd = E.dma("gpsimd", pan_ap(sl),
                                       w2r[e_, fh, j, :, :], pansem[sl])
                        E.wait("tensor", pansem[sl], tok_pd)
                        if j >= 2:
                            E.wait("tensor", "cv", moe_sv[pr])
                        tok_done = None
                        for fcp in range(FH2):
                            for th in range(2):
                                last = (fcp == FH2 - 1 and th == 1)
                                tok_done = E.op(
                                    "tensor",
                                    lambda e, fcp=fcp, th=th, sl=sl, pr=pr: e.matmul(
                                        P[2 * pr + th][:, 0:HT],
                                        pan_ap(sl)[:, fcp * 128:(fcp + 1) * 128],
                                        act_v[:, fcp, th * HT:(th + 1) * HT],
                                        start=(fcp == 0), stop=(fcp == FH2 - 1)),
                                    inc="st" if last else None) or tok_done
                        pan_used[sl] = tok_done
                        if last_pass and j >= 1:
                            moe_stats_chunk(j - 1)
                        E.wait("vector", "st", tok_done)
                        for th in range(2):
                            if first_moe:
                                moe_sv[pr] = E.op(
                                    "vector",
                                    lambda e, j=j, th=th, pr=pr: e.tensor_scalar(
                                        big2[:, j, th * HT:(th + 1) * HT],
                                        P[2 * pr + th][:, 0:HT],
                                        0.0, None, ALU.add))
                            else:
                                moe_sv[pr] = E.op(
                                    "vector",
                                    lambda e, j=j, th=th, pr=pr: e.tensor_tensor(
                                        big2[:, j, th * HT:(th + 1) * HT],
                                        P[2 * pr + th][:, 0:HT],
                                        big2[:, j, th * HT:(th + 1) * HT],
                                        ALU.add))
                        if last_pass:
                            for eb in range(NE):
                                b2_tok[j] = E.op(
                                    "vector",
                                    lambda e, eb=eb, j=j: e.scalar_tensor_tensor(
                                        big2[:, j, :], bcb[:, 2 + eb, :],
                                        b2sb[:, eb, j:j + 1],
                                        big2[:, j, :], ALU.mult, ALU.add))
                    first_moe = False
            moe_stats_chunk(NCH - 1)

            if debug:
                NTC_ = T // 128
                dbg_moe = nc.dram_tensor("dbg_moe", [128, NCH * T], BF16).ap()
                dbg_coeff = nc.dram_tensor("dbg_coeff", [128, 3 * T], BF16).ap()
                dbg_lg = nc.dram_tensor("dbg_lg", [NE, T], F32).ap()
                dbg_gv = nc.dram_tensor("dbg_gv", [128, NTC_ * 8], F32).ap()
                dbg_svec = nc.dram_tensor("dbg_svec", [1, 48], F32).ap()
                dbg_ctm = nc.dram_tensor("dbg_ctm", [128, NTC_ * 4], F32).ap()
                dbg_act = nc.dram_tensor("dbg_act", [128, FH2 * T], BF16).ap()
                dbg_h2 = nc.dram_tensor("dbg_h2", [128, NCH * T], BF16).ap()
                E.wait("sync", "cv", C["cv"])
                E.dma("sync", dbg_moe, big2f[:, :], "dq")
                E.dma("sync", dbg_coeff,
                      bcb.ap().rearrange("p a t -> p (a t)")[:, 2 * T:5 * T], "dq")
                E.dma("sync", dbg_lg, lg32[0:NE, :], "dq")
                E.dma("sync", dbg_gv, gv.ap().rearrange("p a b -> p (a b)"), "dq")
                E.dma("sync", dbg_svec, svec[:, :], "dq")
                E.dma("sync", dbg_ctm, ctm.ap().rearrange("p a b -> p (a b)"), "dq")
                E.dma("sync", dbg_act, ovlf[:, 0:FH2 * T], "dq")
                E.dma("sync", dbg_h2,
                      big1.ap().rearrange("p c t -> p (c t)"), "dq")

            maybe_stop("B")
            tok_r4 = stats_rows(tok_moe_stats, (4, 5, 6, 7))
            bcast_row(mrow, bcb[:, 0, :], tok_r4)
            tok_b1 = bcast_row(rrow, bcb[:, 1, :], tok_r4)

            xs_out = list(xs_after_p1)
            E.wait("vector", "cs2", tok_b1)
            E.wait("sync", "st", C["st"])
            E.wait("sync", "cv", C["cv"])
            E.wait("sync", "cs2", C["cs2"])
            E.wait("gpsimd", "cs2", tok_b1)
            for c in range(NCH):
                sl = c % 4
                eng, sem = eng_of(c)
                tb = tmpb if eng == "vector" else tmpg
                E.wait("sync", xssem[sl], xs_out[sl])
                tok_xl = E.dma("sync", xstg[:, sl, :],
                               x1_dram[c * 128:(c + 1) * 128, :], xssem[sl])
                E.op(eng, lambda e, c=c, tb=tb: e.tensor_tensor(
                    tb[:, :], big2[:, c, :], bcb[:, 1, :], ALU.mult))
                E.op(eng, lambda e, c=c, tb=tb: e.tensor_tensor(
                    tb[:, :], tb[:, :], bcb[:, 0, :], ALU.subtract))
                E.wait(eng, xssem[sl], tok_xl)
                tok_o = E.op(eng, lambda e, c=c, sl=sl, tb=tb: e.tensor_tensor(
                    xstg[:, sl, :], xstg[:, sl, :], tb[:, :], ALU.add))
                E.wait("scalar", sem, tok_o)
                xs_out[sl] = E.dma("scalar", out_fm[c * 128:(c + 1) * 128, :],
                                   xstg[:, sl, :], xssem[sl])
            for _sl in range(4):
                E.wait("sync", xssem[_sl], xs_out[_sl])


        except _StopPlan:
            dump = nc.dram_tensor("dump", [128, NCH * T], BF16).ap()
            for name in ("cv", "cs2", "sg"):
                E.wait("sync", name, E.cnt[name])
            E.wait("sync", "st", E.cnt["st"])
            E.wait("gpsimd", "cv", E.cnt["cv"])
            tok_dmp = E.dma("gpsimd", dump,
                            big1.ap().rearrange("p c t -> p (c t)"), "dgm")
            E.wait("gpsimd", "dgm", E.cnt["dgm"])
            zs = E.op("gpsimd", lambda e: e.memset(xstg[:, 0, :], 0.0))
            E.wait("sync", "sg", zs)
            tokf = [None, None]
            for c in range(NCH):
                tokf[c % 2] = E.dma("sync", out_fm[c * 128:(c + 1) * 128, :],
                                    xstg[:, 0, :], ("dxs0", "dxs1")[c % 2])
            E.wait("sync", "dxs0", tokf[0])
            E.wait("sync", "dxs1", tokf[1])

        block = st_.enter_context(nc.Block())

        @block.sync
        def _(eng):
            for f in E.q["sync"]:
                f(eng)

        @block.gpsimd
        def _(eng):
            for f in E.q["gpsimd"]:
                f(eng)

        @block.vector
        def _(eng):
            for f in E.q["vector"]:
                f(eng)

        @block.scalar
        def _(eng):
            for f in E.q["scalar"]:
                f(eng)

        @block.tensor
        def _(eng):
            for f in E.q["tensor"]:
                f(eng)

    return nc


def _prep_inputs(L0_attn_data, L0_attn_mask, L0_attn_weight, L0_attn_bias,
                 L0_ffn_gate_w, L0_ffn_gate_b, L0_ffn_w1, L0_ffn_b1,
                 L0_ffn_w2, L0_ffn_b2, D, T, S):
    NCH = D // 128
    FH2 = NCH // 2
    VIS = int(S * 0.3)
    cpb = S // T

    x = np.ascontiguousarray(np.asarray(L0_attn_data, np.float32)).reshape(-1, D)
    assert x.shape[0] == N_CORES * T

    w = np.asarray(L0_attn_weight, np.float32)
    wtr = np.ascontiguousarray(
        w.reshape(NCH, 128, 2, FH2, 128).transpose(0, 2, 4, 3, 1)
    ).reshape(NCH, 2, 128, FH2 * 128)

    w1 = np.asarray(L0_ffn_w1, np.float32)
    w1r = np.ascontiguousarray(
        w1.reshape(NE, 2, FH2, 128, NCH, 128).transpose(0, 4, 1, 3, 2, 5)
    ).reshape(NE, NCH, 2, 128, FH2 * 128)

    w2 = np.asarray(L0_ffn_w2, np.float32)
    w2r = np.ascontiguousarray(
        w2.reshape(NE, 2, FH2, 128, NCH, 128).transpose(0, 1, 4, 3, 2, 5)
    ).reshape(NE, 2, NCH, 128, FH2 * 128)

    b1 = np.asarray(L0_ffn_b1, np.float32)
    b1r = np.ascontiguousarray(b1.reshape(NE, NCH, 128).transpose(2, 0, 1))
    b2 = np.asarray(L0_ffn_b2, np.float32)
    b2cv = np.ascontiguousarray(b2.reshape(NE, NCH, 128).transpose(2, 0, 1))
    bias = np.asarray(L0_attn_bias, np.float32)
    biascv = np.ascontiguousarray(bias.reshape(NCH, 128).T)

    gw = np.asarray(L0_ffn_gate_w, np.float32)
    gwr = np.ascontiguousarray(gw[:D].reshape(NCH, 128, NE).transpose(1, 0, 2))
    gwlast = np.ascontiguousarray(gw[D:D + 1, :])
    gb = np.ascontiguousarray(np.asarray(L0_ffn_gate_b, np.float32)[None, :])

    mask = np.ascontiguousarray(np.asarray(L0_attn_mask, np.float32))

    shared = dict(wtr=wtr, w1r=w1r, w2r=w2r, b1r=b1r, b2c=b2cv,
                  biasc=biascv, gwr=gwr, gwlast=gwlast, gbias=gb)

    in_maps = []
    for c in range(N_CORES):
        shard = x[c * T:(c + 1) * T, :]
        x_fm = np.ascontiguousarray(shard.T)
        s0 = (c % cpb) * T
        spos = s0 + np.arange(T)
        vis = (spos < VIS).astype(ml_dtypes.bfloat16)[None, :]
        mys = np.zeros((1, N_CORES), np.float32)
        mys[0, (c // cpb) * cpb:(c // cpb + 1) * cpb] = 1.0
        t0s = np.zeros((1, N_CORES), np.float32)
        t0s[0, (c // cpb) * cpb] = 1.0
        m = dict(shared)
        m.update(x_fm=x_fm, visrow=np.ascontiguousarray(vis),
                 maskd=mask[c * (D // N_CORES):(c + 1) * (D // N_CORES)],
                 mysel=mys, tok0sel=t0s)
        in_maps.append(m)
    return in_maps


def run(inputs, D, T, S, nc=None):
    in_maps = _prep_inputs(D=D, T=T, S=S, **inputs)
    if nc is None:
        nc = build(D=D, T=T, S=S)
    res = run_bass_kernel_spmd(nc, in_maps, core_ids=list(range(N_CORES)))
    shards = [np.asarray(res.results[c]["out_fm"], np.float32).T
              for c in range(N_CORES)]
    return np.concatenate(shards, axis=0)


def kernel(**inputs):
    D, T, S = 4096, 1024, 4096
    out = run(inputs, D, T, S)
    B = inputs["L0_attn_data"].shape[0]
    return np.ascontiguousarray(out.reshape(B, S, D).astype(np.float32))
